# version 04
# baseline (speedup 1.0000x reference)
import math
from contextlib import ExitStack

import numpy as np
import ml_dtypes

import concourse.bass as bass
import concourse.mybir as mybir
from concourse.bass_utils import run_bass_kernel_spmd

F32 = mybir.dt.float32
BF16 = mybir.dt.bfloat16
I32 = mybir.dt.int32
AF = mybir.ActivationFunctionType
ALU = mybir.AluOpType
AX = mybir.AxisListType

ENGINES = ("tensor", "vector", "scalar", "gpsimd", "sync")

T = 2048
D = 2048
NH = 16
MAGIC = float(np.float32(1.5 * 2 ** 23))
LAMBDA_INIT = 0.8 - 0.6 * math.exp(-0.3)
NEGBIG = -1.0e9
DBG = False


class _Chain:
    __slots__ = ("entry",)

    def __init__(self, entry):
        self.entry = entry

    def then_inc(self, sem, n):
        self.entry[3].append((sem, n))
        return self


def _is_ap(x):
    return type(x).__name__ == "AP"


_OUT_KW = ("out", "accum_out", "out_ap", "dst")
_SEQ_OPS = ("wait_ge", "sem_inc", "nop")


def _op_io(name, a, kw):
    writes, reads = [], []
    if name == "memset" or name == "memzero":
        writes.append(a[0] if a else kw.get("ap"))
        return writes, reads
    if name == "dma_start":
        o = kw.get("out", a[0] if len(a) > 0 else None)
        i = kw.get("in_", a[1] if len(a) > 1 else None)
        if _is_ap(o):
            writes.append(o)
        if _is_ap(i):
            reads.append(i)
        return writes, reads
    first = True
    for x in a:
        if _is_ap(x):
            if first:
                writes.append(x)
                first = False
            else:
                reads.append(x)
    for k, x in kw.items():
        if _is_ap(x):
            if k in _OUT_KW:
                writes.append(x)
            else:
                reads.append(x)
    return writes, reads


class Rec:

    def __init__(self, name):
        self._name = name
        self.calls = []
        self._kb = None
        self._waited = {}

    def __getattr__(self, name):
        if name.startswith("_"):
            raise AttributeError(name)

        def call(*a, **kw):
            entry = [name, a, kw, [], None, self._name]
            if name not in _SEQ_OPS and self._kb is not None:
                self._kb._track_op(self, entry)
            self.calls.append(entry)
            if self._kb is not None and name not in _SEQ_OPS \
                    and name != "dma_start":
                self._kb._pending[self._name] = entry
            return _Chain(entry)

        return call

    def _note_wait(self, sem, val):
        if self._waited.get(id(sem), -1) < val:
            self._waited[id(sem)] = val

    def _emit_wait(self, sem, val):
        if self._waited.get(id(sem), -1) < val:
            self.calls.append(["wait_ge", (sem, val), {}, [], None, self._name])
            self._waited[id(sem)] = val

    def _replay(self, eng):
        for name, a, kw, incs, _tok, _e in self.calls:
            inst = getattr(eng, name)(*a, **kw)
            for sem, n in incs:
                inst = inst.then_inc(sem, n)

    def _last_real(self, since):
        for i in range(len(self.calls) - 1, since - 1, -1):
            if self.calls[i][0] not in ("wait_ge", "sem_inc", "dma_start"):
                return i
        return None


class Edge:

    __slots__ = ("sem", "cnt", "queue")

    def __init__(self, sem):
        self.sem = sem
        self.cnt = 0

    def inc(self, chain, n=1):
        chain.then_inc(self.sem, n)
        self.cnt += n
        if chain.entry[4] is None:
            chain.entry[4] = (self.sem, self.cnt)
        return self.cnt

    def wait(self, rec, cnt=None):
        rec.wait_ge(self.sem, self.cnt if cnt is None else cnt)


class KB:

    def __init__(self, nc):
        self.nc = nc
        self.es = ExitStack()
        self.r = {e: Rec(e) for e in ENGINES}
        self.t = self.r["tensor"]
        self.v = self.r["vector"]
        self.s = self.r["scalar"]
        self.g = self.r["gpsimd"]
        self.y = self.r["sync"]
        self.bar = self.sem("bar")
        self.nphase = 0
        self._phase_start = {e: 0 for e in ENGINES}
        self.dmasem = {e: self.sem("dma_" + e) for e in ("gpsimd", "sync", "scalar")}
        self.dmacnt = {e: 0 for e in ("gpsimd", "sync", "scalar")}
        self.fin = {e: Edge(self.sem("fin_" + e)) for e in ENGINES}
        self._edges = {}
        self._dsems = []
        self._lastw = {}
        self._lastr = {}
        self._pending = {e: None for e in ENGINES}
        self._autogen = {}
        for e in ENGINES:
            self.r[e]._kb = self
        self.dpool = {e: [self.dsem(f"dp_{e}{i}", e) for i in range(6)]
                      for e in ("gpsimd", "sync", "scalar")}
        self.dpi = {e: 0 for e in ("gpsimd", "sync", "scalar")}

    def sem(self, name=None):
        return self.es.enter_context(self.nc.semaphore(name))

    def edge(self, name):
        if name not in self._edges:
            self._edges[name] = Edge(self.sem(name))
        return self._edges[name]

    def _sem_count(self, sem):
        for ed in list(self._edges.values()) + list(self.fin.values()):
            if ed.sem is sem:
                return ed.cnt
        for e, s in self.dmasem.items():
            if s is sem:
                return self.dmacnt[e]
        raise KeyError("untracked semaphore in barrier")

    def sbuf(self, name, shape, dtype=F32, scope=None):
        return (scope or self.es).enter_context(self.nc.sbuf_tensor(name, shape, dtype))

    def psum(self, name, shape, dtype=F32, scope=None):
        return (scope or self.es).enter_context(self.nc.psum_tensor(name, shape, dtype))

    def dram(self, name, shape, dtype=F32, kind="Internal"):
        return self.nc.dram_tensor(name, shape, dtype, kind=kind).ap()

    def dsem(self, name, queue):
        d = Edge(self.sem(name))
        d.queue = queue
        self._dsems.append(d)
        return d

    SEM_CAP = 20000

    def _auto_edge(self, e):
        key = "auto_" + e
        gen = self._autogen.get(e, 0)
        ed = self.edge(key + (str(gen) if gen else ""))
        if ed.cnt >= self.SEM_CAP:
            gen += 1
            self._autogen[e] = gen
            ed = self.edge(key + str(gen))
        return ed

    def _finalize_pending(self, e):
        ent = self._pending.get(e)
        if ent is None:
            return
        self._pending[e] = None
        if ent[4] is not None:
            return
        if ent[3]:
            raise RuntimeError(f"producer {ent[0]}@{e} has untracked inc")
        au = self._auto_edge(e)
        ent[3].append((au.sem, 1))
        au.cnt += 1
        ent[4] = (au.sem, au.cnt)

    def _token(self, ent):
        if ent[4] is None:
            self._finalize_pending(ent[5])
        tok = ent[4]
        assert tok is not None, f"untokened producer {ent[0]}@{ent[5]}"
        return tok

    def _track_op(self, rec, entry):
        self._finalize_pending(rec._name)
        name, a, kw = entry[0], entry[1], entry[2]
        ws, rs = _op_io(name, a, kw)
        ws = [x for x in ws if x is not None and str(x.space.value) != "dram"]
        rs = [x for x in rs if x is not None and str(x.space.value) != "dram"]
        deps = []
        for x in rs + ws:
            deps.extend(self._lastw.get(x.tensor.name, ()))
        for x in ws:
            deps.extend(self._lastr.get(x.tensor.name, ()))
        seen = set()
        for ent in deps:
            if id(ent) in seen or ent is entry:
                continue
            seen.add(id(ent))
            if ent[5] == "tensor" and rec._name == "tensor":
                continue
            sem, val = self._token(ent)
            full = self._dma_full_count(sem)
            if full is not None:
                val = full
            rec._emit_wait(sem, val)
        for w in ws:
            self._lastw.setdefault(w.tensor.name, []).append(entry)
        for x in rs:
            self._lastr.setdefault(x.tensor.name, []).append(entry)

    def _dma_full_count(self, sem):
        for e, s in self.dmasem.items():
            if s is sem:
                return self.dmacnt[e]
        for d in self._dsems:
            if d.sem is sem:
                return d.cnt
        return None

    def dma(self, e, out, in_, ds=None, **kw):
        if kw.get("transpose"):
            e = "sync"
        if ds is None:
            ds = self.dpool[e][self.dpi[e]]
            self.dpi[e] = (self.dpi[e] + 1) % len(self.dpool[e])
        if ds.cnt > 0:
            self.r[e].wait_ge(ds.sem, ds.cnt)
            self.r[e]._note_wait(ds.sem, ds.cnt)
        ch = self.r[e].dma_start(out=out, in_=in_, **kw)
        ch.then_inc(ds.sem, 16)
        ds.cnt += 16
        ch.entry[4] = (ds.sem, ds.cnt)
        return ds.cnt

    def wait_dma(self, consumer, e, cnt=None):
        v = self.dmacnt[e] if cnt is None else cnt
        self.r[consumer].wait_ge(self.dmasem[e], v)
        self.r[consumer]._note_wait(self.dmasem[e], v)

    def barrier(self):
        self.nphase += 1
        tgt = 5 * self.nphase
        for e in ENGINES:
            r = self.r[e]
            for d in self._dsems:
                if d.queue == e and d.cnt > 0:
                    r.wait_ge(d.sem, d.cnt)
            li = r._last_real(self._phase_start[e])
            if li is not None:
                sem, val = self._token(r.calls[li])
                full = self._dma_full_count(sem)
                r.wait_ge(sem, full if full is not None else val)
            r.sem_inc(self.bar, 1)
        for e in ENGINES:
            self.r[e].wait_ge(self.bar, tgt)
            self._phase_start[e] = len(self.r[e].calls)
        self._lastw.clear()
        self._lastr.clear()

    def finalize(self):
        nc = self.nc
        recs = self.r
        with nc.Block() as block:
            @block.tensor
            def _(eng):
                recs["tensor"]._replay(eng)

            @block.vector
            def _(eng):
                recs["vector"]._replay(eng)

            @block.scalar
            def _(eng):
                recs["scalar"]._replay(eng)

            @block.gpsimd
            def _(eng):
                recs["gpsimd"]._replay(eng)

            @block.sync
            def _(eng):
                recs["sync"]._replay(eng)
        self.es.close()
        return nc


class MMStage:

    def __init__(self, kb, banks, name, evac_shape=None, evac_dtype=BF16,
                 dma_q="scalar", scope=None):
        self.kb = kb
        self.banks = banks
        self.n = 0
        self.ebufs = None
        if evac_shape is not None:
            self.ebufs = [kb.sbuf(f"{name}_eb{i}", evac_shape, evac_dtype,
                                  scope=scope)
                          for i in range(2)]
            self.eds = [kb.dsem(f"{name}_ds{i}", dma_q) for i in range(2)]
        self.dma_q = dma_q

    def tile_custom(self, emit_mms, emit_evac):
        bank = self.banks[self.n % len(self.banks)]
        emit_mms(bank)
        emit_evac(bank)
        self.n += 1

    def tile(self, emit_mms, dst_ap):
        kb = self.kb
        i = self.n
        eb = self.ebufs[i % 2]
        bank = self.banks[i % len(self.banks)]
        emit_mms(bank)
        kb.v.tensor_copy(eb[:], bank[:])
        kb.dma(self.dma_q, dst_ap, eb[:], ds=self.eds[i % 2])
        self.n += 1


def emit_ln_rows(kb, x_ap, out_ap, width, eps, tmp, stats):
    v, s = kb.v, kb.s
    tmp = tmp[:, 0:width]
    s1, s2 = stats[:, 0:1], stats[:, 1:2]
    mu, var = stats[:, 2:3], stats[:, 3:4]
    rstd = stats[:, 4:5]
    v.tensor_reduce(s1, x_ap, AX.X, ALU.add)
    s.activation(tmp, x_ap, AF.Square, accum_out=s2)
    v.tensor_scalar(mu, s1, 1.0 / width, None, ALU.mult)
    v.tensor_scalar(s2, s2, 1.0 / width, None, ALU.mult)
    v.tensor_tensor(var, mu, mu, ALU.mult)
    v.tensor_tensor(var, s2, var, ALU.subtract)
    s.activation(rstd, var, AF.Ln, bias=float(eps))
    s.activation(rstd, rstd, AF.Exp, scale=-0.5)
    return v.tensor_scalar(out_ap, x_ap, mu, rstd, ALU.subtract, ALU.mult)


def build_l0_attn(kb, x_in, wqkv, wproj, lam_in, x1_out, cs):
    t, v, s = kb.t, kb.v, kb.s
    dmask = cs["dmask"]
    eh2 = cs["eh2"]
    selrow = cs["selrow"]
    ones_col = cs["ones_col"]
    ones_colb = cs["ones_colb"]
    ones_row = cs["ones_row"]

    dk = "ExternalOutput" if DBG else "Internal"
    scr_xbf = kb.dram("scr_xbf", [T, D], BF16, kind=dk)
    scr_qkT = kb.dram("scr_qkT", [4096, T], BF16, kind=dk)
    scr_vbf = kb.dram("scr_vbf", [T, D], BF16, kind=dk)
    scr_difT = kb.dram("scr_difT", [D, T], BF16, kind=dk)

    kb.dma("gpsimd", scr_xbf, x_in)
    lamv = kb.sbuf("lamv", [1, 256], F32)
    kb.dma("sync", lamv[:], lam_in)
    kb.barrier()

    sc = ExitStack()
    xT = kb.sbuf("xT", [128, 16 * T], BF16, scope=sc)
    for k in range(16):
        kb.dma("sync", xT[:, k * T:(k + 1) * T],
               scr_xbf[:, k * 128:(k + 1) * 128], transpose=True)
    wslab = [kb.sbuf(f"wslab{i}", [128, 16 * 512], BF16, scope=sc) for i in range(2)]
    ws_ds = [kb.dsem(f"ws_ds{i}", "gpsimd") for i in range(2)]
    kb.barrier()

    banks = [kb.psum("psq%d" % i, [128, 512], F32, scope=sc) for i in range(4)]
    ebq = [kb.sbuf("qkv_eb%d" % i, [128, 512], BF16, scope=sc) for i in range(2)]
    eb_ds = kb.dsem("qkv_ebds", "scalar")
    for slab in range(8):
        wb = wslab[slab % 2]
        n0 = slab * 512
        kb.dma("gpsimd", wb[:].rearrange("p (k n) -> p k n", k=16),
               wqkv[:, n0:n0 + 512].rearrange("(k p) n -> p k n", p=128),
               ds=ws_ds[slab % 2])
        for m in range(4):
            of_abs = slab * 4 + m
            for k in range(16):
                for n in range(4):
                    t.matmul(banks[n][:],
                             wb[:, (k * 512 + m * 128):(k * 512 + m * 128 + 128)],
                             xT[:, (k * T + n * 512):(k * T + n * 512 + 512)],
                             start=(k == 0), stop=(k == 15),
                             skip_group_check=True)
            for n in range(4):
                eb = ebq[(m * 4 + n) % 2]
                v.tensor_copy(eb[:], banks[n][:])
                kb.dma("scalar", scr_qkT[of_abs * 128:(of_abs + 1) * 128,
                                         n * 512:(n + 1) * 512], eb[:], ds=eb_ds)
    wv4 = kb.sbuf("qkv_wv4", [128, 16 * T], BF16, scope=sc)
    kb.dma("gpsimd", wv4[:].rearrange("p (k n) -> p k n", k=16),
           wqkv[:, 4096:6144].rearrange("(k p) n -> p k n", p=128))
    for m in range(16):
        for k in range(16):
            for n in range(4):
                t.matmul(banks[n][:],
                         xT[:, (k * T + m * 128):(k * T + m * 128 + 128)],
                         wv4[:, k * T + n * 512:k * T + n * 512 + 512],
                         start=(k == 0), stop=(k == 15), skip_group_check=True)
        for n in range(4):
            eb = ebq[(m * 4 + n) % 2]
            v.tensor_copy(eb[:], banks[n][:])
            kb.dma("scalar", scr_vbf[m * 128:(m + 1) * 128,
                                     n * 512:(n + 1) * 512], eb[:], ds=eb_ds)
    kb.barrier()
    sc.close()

    scd = ExitStack()
    neglam = kb.sbuf("neglam", [128, 1], F32, scope=scd)
    negM = kb.sbuf("negM", [128, 32], F32, scope=scd)
    sc = ExitStack()
    sm = kb.sbuf("sm_l0", [1, 64], F32, scope=sc)
    qk_sb = kb.sbuf("qk_sb", [128, 2048], BF16, scope=sc)
    sqf = kb.sbuf("sqf", [128, 2048], F32, scope=sc)
    mx = kb.sbuf("mx_l0", [2, 48], F32, scope=sc)
    pm = kb.psum("ps_mh", [128, 512], F32, scope=sc)
    pbc = kb.psum("ps_bc", [128, 32], F32, scope=sc)

    v.tensor_tensor(sm[:, 0:64], lamv[:, 0:64], lamv[:, 64:128], ALU.mult)
    v.tensor_reduce(mx[0:1, 40:41], sm[:, 0:64], AX.X, ALU.add)
    v.tensor_tensor(sm[:, 0:64], lamv[:, 128:192], lamv[:, 192:256], ALU.mult)
    v.tensor_reduce(mx[0:1, 41:42], sm[:, 0:64], AX.X, ALU.add)
    s.activation(mx[0:1, 42:44], mx[0:1, 40:42], AF.Exp)
    v.tensor_tensor(mx[0:1, 44:45], mx[0:1, 42:43], mx[0:1, 43:44], ALU.subtract)
    v.tensor_scalar(mx[0:1, 45:46], mx[0:1, 44:45], float(LAMBDA_INIT),
                    -1.0, ALU.add, ALU.mult)
    t.matmul(pbc[:, 0:1], ones_row[:], mx[0:1, 45:46], start=True, stop=True)
    v.tensor_copy(neglam[:], pbc[:, 0:1])

    for tile in range(32):
        kb.dma("sync", qk_sb[:], scr_qkT[tile * 128:(tile + 1) * 128, :])
        s.activation(sqf[:], qk_sb[:], AF.Square)
        for n in range(4):
            t.matmul(pm[0:2, 0:512], eh2[:], sqf[:, n * 512:(n + 1) * 512],
                     start=True, stop=True)
            v.tensor_reduce(mx[0:2, 46:47], pm[0:2, 0:512], AX.X, ALU.max)
            if n == 0:
                v.tensor_copy(mx[0:2, tile:tile + 1], mx[0:2, 46:47])
            else:
                v.tensor_tensor(mx[0:2, tile:tile + 1], mx[0:2, tile:tile + 1],
                                mx[0:2, 46:47], ALU.max)
    for j in range(2):
        q0, k0 = 16 * j, 16 * j + 8
        v.tensor_tensor(mx[0:2, 32:40], mx[0:2, q0:q0 + 8],
                        mx[0:2, k0:k0 + 8], ALU.mult)
        s.activation(mx[0:2, 32:40], mx[0:2, 32:40], AF.Sqrt, scale=1.0 / 64)
        v.tensor_scalar(mx[0:2, 32:40], mx[0:2, 32:40], -1.0, None, ALU.mult)
        for r in range(2):
            t.matmul(pbc[:, 16 * j + 8 * r:16 * j + 8 * r + 8],
                     selrow[:, 128 * r:128 * (r + 1)], mx[0:2, 32:40],
                     start=True, stop=True)
    v.tensor_copy(negM[:], pbc[:])
    if DBG:
        dbgM = kb.dram("dbgM", [128, 32], F32, kind="ExternalOutput")
        kb.dma("gpsimd", dbgM, negM[:])
    kb.barrier()
    sc.close()

    sc = ExitStack()
    kT = kb.sbuf("kT_l0", [128, 16 * T], BF16, scope=sc)
    vsb = kb.sbuf("v_l0", [128, 256 * 128], BF16, scope=sc)
    qsl = kb.sbuf("q_l0", [128, 2 * 512], BF16, scope=sc)
    psb = [kb.sbuf("p_l0%d" % i, [128, 512], BF16, scope=sc) for i in range(2)]
    u_sb = kb.sbuf("u_l0", [128, 512], F32, scope=sc)
    t2_sb = kb.sbuf("t2_l0", [128, 512], F32, scope=sc)
    c1_sb = kb.sbuf("c1_l0", [128, 512], F32, scope=sc)
    c2_sb = kb.sbuf("c2_l0", [128, 512], F32, scope=sc)
    row_sb = kb.sbuf("row_l0", [1, 512], F32, scope=sc)
    dif_bf = [kb.sbuf("dif_l0%d" % i, [128, 512], BF16, scope=sc) for i in range(2)]
    dif_ds = [kb.dsem("dif_ds%d" % i, "scalar") for i in range(2)]
    for i in range(8):
        kb.dma("sync", kT[:, i * T:(i + 1) * T],
               scr_qkT[(8 + i) * 128:(9 + i) * 128, :])
        kb.dma("sync", kT[:, (8 + i) * T:(9 + i) * T],
               scr_qkT[(24 + i) * 128:(25 + i) * 128, :])
    for h in range(16):
        kb.dma("scalar", vsb[:, h * 16 * 128:(h + 1) * 16 * 128]
               .rearrange("p (c f) -> p c f", c=16),
               scr_vbf[:, h * 128:(h + 1) * 128]
               .rearrange("(c p) f -> p c f", p=128))
    kb.barrier()

    psS = [kb.psum("psS%d" % i, [128, 512], F32, scope=sc) for i in range(2)]
    psA1 = kb.psum("psA1", [128, 512], F32, scope=sc)
    psA2 = kb.psum("psA2", [128, 512], F32, scope=sc)
    psl1 = kb.psum("psl1", [1, 512], F32, scope=sc)
    psl2 = kb.psum("psl2", [1, 512], F32, scope=sc)
    psE = kb.psum("psE", [128, 512], F32, scope=sc)
    qs_ds = kb.dsem("qs_ds", "sync")

    for h in range(16):
        ktile = h // 2
        kbase = 64 * (h % 2)
        for tq in range(4):
            kb.dma("sync", qsl[kbase:kbase + 64, 0:512],
                   scr_qkT[h * 64:h * 64 + 64, tq * 512:(tq + 1) * 512],
                   ds=qs_ds)
            kb.dma("sync", qsl[kbase:kbase + 64, 512:1024],
                   scr_qkT[2048 + h * 64:2048 + h * 64 + 64,
                           tq * 512:(tq + 1) * 512], ds=qs_ds)
            nch = 4 * tq + 4
            for j in range(2):
                psA = psA1 if j == 0 else psA2
                psl = psl1 if j == 0 else psl2
                for c in range(nch):
                    pi = (j * nch + c) % 2
                    lhsT = kT[kbase:kbase + 64,
                              (8 * j + ktile) * T + c * 128:
                              (8 * j + ktile) * T + c * 128 + 128]
                    rhs = qsl[kbase:kbase + 64, j * 512:j * 512 + 512]
                    t.matmul(psS[pi][:], lhsT, rhs, start=True, stop=True,
                             skip_group_check=True)
                    dloc = c - (nch - 4)
                    if dloc >= 0:
                        v.tensor_tensor(psS[pi][:], psS[pi][:],
                                        dmask[:, dloc * 512:(dloc + 1) * 512],
                                        ALU.add)
                    bcol = j * 16 + (h % 2) * 8 + h // 2
                    s.activation(psb[pi][:], psS[pi][:], AF.Exp,
                                 bias=negM[:, bcol:bcol + 1], scale=0.125)
                    if DBG and (h, tq, j, c) == (0, 0, 0, 0):
                        dbgP = kb.dram("dbgP", [128, 512], BF16,
                                       kind="ExternalOutput")
                        kb.dma("gpsimd", dbgP, psb[pi][:])
                    t.matmul(psA[:], vsb[:, (h * 16 + c) * 128:
                                         (h * 16 + c + 1) * 128],
                             psb[pi][:], start=(c == 0), stop=(c == nch - 1),
                             skip_group_check=True)
                    t.matmul(psl[:], ones_colb[:], psb[pi][:],
                             start=(c == 0), stop=(c == nch - 1),
                             skip_group_check=True)
            for j in range(2):
                v.reciprocal(row_sb[:], (psl1 if j == 0 else psl2)[:])
                t.matmul(psE[:], ones_row[:], row_sb[:], start=True, stop=True)
                s.activation(c1_sb[:] if j == 0 else c2_sb[:], psE[:], AF.Copy)
            v.tensor_tensor(u_sb[:], psA1[:], c1_sb[:], ALU.mult)
            v.tensor_tensor(t2_sb[:], psA2[:], c2_sb[:], ALU.mult)
            v.scalar_tensor_tensor(u_sb[:], t2_sb[:], neglam[:, 0:1],
                                   u_sb[:], ALU.mult, ALU.add)
            s.activation(t2_sb[:], u_sb[:], AF.Square)
            t.matmul(psE[0:1, :], ones_col[:], t2_sb[:], start=True, stop=True)
            s.activation(row_sb[:], psE[0:1, :], AF.Ln, bias=1e-5,
                         scale=1.0 / 128)
            s.activation(row_sb[:], row_sb[:], AF.Exp, scale=-0.5)
            t.matmul(psE[:], ones_row[:], row_sb[:], start=True, stop=True)
            s.activation(c1_sb[:], psE[:], AF.Copy,
                         scale=float(1.0 - LAMBDA_INIT))
            bi = (h * 4 + tq) % 2
            db = dif_bf[bi]
            v.tensor_tensor(db[:], u_sb[:], c1_sb[:], ALU.mult)
            kb.dma("scalar", scr_difT[h * 128:(h + 1) * 128,
                                      tq * 512:(tq + 1) * 512], db[:],
                   ds=dif_ds[bi])
    kb.barrier()
    sc.close()
    scd.close()

    sc = ExitStack()
    difT = kb.sbuf("difT_sb", [128, 16 * T], BF16, scope=sc)
    wpj = kb.sbuf("wpj", [128, 16 * T], BF16, scope=sc)
    xres = kb.sbuf("xres_l0", [128, 2048], F32, scope=sc)
    rrow = kb.sbuf("rrow_l0", [128, 2048], F32, scope=sc)
    ltmp = kb.sbuf("ltmp_l0", [128, 2048], F32, scope=sc)
    lstat = kb.sbuf("lstat_l0", [128, 8], F32, scope=sc)
    xres_ds = kb.dsem("xres_ds", "sync")
    x1_ds = kb.dsem("x1_ds", "gpsimd")
    for k in range(16):
        kb.dma("sync", difT[:, k * T:(k + 1) * T],
               scr_difT[k * 128:(k + 1) * 128, :])
    kb.dma("gpsimd", wpj[:].rearrange("p (k n) -> p k n", k=16),
           wproj.rearrange("(k p) n -> p k n", p=128))
    kb.barrier()
    banks = [kb.psum("pspj%d" % i, [128, 512], F32, scope=sc) for i in range(4)]
    for m in range(16):
        kb.dma("sync", xres[:], x_in[m * 128:(m + 1) * 128, :], ds=xres_ds)
        for k in range(16):
            for n in range(4):
                t.matmul(banks[n][:],
                         difT[:, k * T + m * 128:k * T + m * 128 + 128],
                         wpj[:, k * T + n * 512:k * T + n * 512 + 512],
                         start=(k == 0), stop=(k == 15), skip_group_check=True)
        for n in range(4):
            v.tensor_tensor(rrow[:, n * 512:(n + 1) * 512], banks[n][:],
                            xres[:, n * 512:(n + 1) * 512], ALU.add)
        emit_ln_rows(kb, rrow[:], ltmp[:], 2048, 1e-5, ltmp, lstat)
        kb.dma("gpsimd", x1_out[m * 128:(m + 1) * 128, :], ltmp[:, 0:2048],
               ds=x1_ds)
    kb.barrier()
    sc.close()


def emit_actquant_rows(kb, x_ap, out_bf_ap, sinv_col, qs, W, eps=1e-5):
    v, g = kb.v, kb.g
    amax, r = qs["st"][:, 0:1], qs["st"][:, 1:2]
    s127 = qs["st"][:, 2:3]
    t_, nf = qs["t"][:, 0:W], qs["nf"][:, 0:W]
    b1, b2 = qs["b1"][:, 0:W], qs["b2"][:, 0:W]
    i32 = qs["i32"][:, 0:W]
    v.tensor_reduce(amax, x_ap, AX.X, ALU.max, apply_absolute_value=True)
    v.tensor_scalar(amax, amax, float(eps), None, ALU.max)
    v.reciprocal(r, amax)
    v.tensor_scalar(s127, r, 127.0, None, ALU.mult)
    v.tensor_scalar(sinv_col, amax, 1.0 / 127.0, None, ALU.mult)
    v.tensor_scalar(t_, x_ap, s127, None, ALU.mult)
    g.tensor_scalar(nf, t_, 0.5 + MAGIC, MAGIC, ALU.add, ALU.subtract)
    g.tensor_copy(i32, nf)
    g.tensor_copy(nf, i32)
    v.scalar_tensor_tensor(b1, t_, 1.5, nf, ALU.add, ALU.is_ge)
    v.scalar_tensor_tensor(b2, t_, 0.5, nf, ALU.add, ALU.is_ge)
    v.tensor_tensor(b1, b1, b2, ALU.add)
    v.scalar_tensor_tensor(b2, t_, -0.5, nf, ALU.add, ALU.is_ge)
    v.scalar_tensor_tensor(b1, b2, 2.0, b1, ALU.subtract, ALU.add)
    return v.tensor_tensor(out_bf_ap, b1, nf, ALU.add)


def emit_tern_group(kb, w_dram, out_dram, O, I, sc, tag, eps=1e-5):
    v = kb.v
    ng = I // 128
    wt = kb.sbuf(f"tw_{tag}", [128, I], F32, scope=sc)
    a = kb.sbuf(f"ta_{tag}", [128, I], F32, scope=sc)
    b = kb.sbuf(f"tb_{tag}", [128, I], F32, scope=sc)
    wq = kb.sbuf(f"tq_{tag}", [128, I], BF16, scope=sc)
    ss = kb.sbuf(f"ts_{tag}", [128, 3 * ng], F32, scope=sc)
    wq_ds = kb.dsem(f"twd_{tag}", "scalar")
    for rt in range(O // 128):
        kb.dma("sync", wt[:], w_dram[rt * 128:(rt + 1) * 128, :])
        w3 = wt[:].rearrange("p (a b) -> p a b", a=ng)
        sums = ss[:, 0:ng]
        thr = ss[:, ng:2 * ng]
        nthr = ss[:, 2 * ng:3 * ng]
        v.tensor_reduce(sums, w3, AX.X, ALU.add, apply_absolute_value=True)
        v.tensor_scalar(thr, sums, 1.0 / 256.0, float(eps / 2), ALU.mult, ALU.add)
        v.tensor_scalar(nthr, thr, -1.0, None, ALU.mult)
        a3 = a[:].rearrange("p (a b) -> p a b", a=ng)
        b3 = b[:].rearrange("p (a b) -> p a b", a=ng)
        v.tensor_tensor(a3, w3, thr.unsqueeze(-1).broadcast_to([128, ng, 128]),
                        ALU.is_ge)
        v.tensor_tensor(b3, w3, nthr.unsqueeze(-1).broadcast_to([128, ng, 128]),
                        ALU.is_ge)
        v.tensor_tensor(a[:], a[:], b[:], ALU.add)
        v.tensor_scalar(thr, thr, 2.0, None, ALU.mult)
        v.scalar_tensor_tensor(
            wq[:].rearrange("p (a b) -> p a b", a=ng), a3, 1.0,
            thr.unsqueeze(-1).broadcast_to([128, ng, 128]),
            ALU.subtract, ALU.mult)
        kb.dma("scalar", out_dram[rt * 128:(rt + 1) * 128, :], wq[:], ds=wq_ds)


def build_l0_ffn(kb, x1_in, w_gate, w_up, w_down, x2_out, cs):
    t, v, s, g = kb.t, kb.v, kb.s, kb.g
    DFF = 5504
    NKD = DFF // 128
    dk = "ExternalOutput" if DBG else "Internal"
    scr_xq = kb.dram("f0_xq", [T, D], BF16, kind=dk)
    scr_wgq = kb.dram("f0_wgq", [DFF, D], BF16, kind=dk)
    scr_wuq = kb.dram("f0_wuq", [DFF, D], BF16, kind=dk)
    scr_wdq = kb.dram("f0_wdq", [D, DFF], BF16, kind=dk)
    scr_hid = kb.dram("f0_hid", [T, DFF], F32, kind=dk)
    scr_hq = kb.dram("f0_hq", [T, DFF], BF16, kind=dk)
    scr_pd = kb.dram("f0_pd", [T, D], F32, kind=dk)
    sinv_x = kb.sbuf("f0_sinvx", [128, 16], F32)
    sinv_h = kb.sbuf("f0_sinvh", [128, 16], F32)

    sc = ExitStack()
    xrow = kb.sbuf("f0_xrow", [128, D], F32, scope=sc)
    qs = {"t": kb.sbuf("f0_qt", [128, D], F32, scope=sc),
          "nf": kb.sbuf("f0_qn", [128, D], F32, scope=sc),
          "b1": kb.sbuf("f0_qb1", [128, D], F32, scope=sc),
          "b2": kb.sbuf("f0_qb2", [128, D], F32, scope=sc),
          "i32": kb.sbuf("f0_qi", [128, D], I32, scope=sc),
          "st": kb.sbuf("f0_qs", [128, 4], F32, scope=sc)}
    xq_bf = kb.sbuf("f0_xqb", [128, D], BF16, scope=sc)
    xq_ds = kb.dsem("f0_xqds", "scalar")
    for m in range(16):
        kb.dma("sync", xrow[:], x1_in[m * 128:(m + 1) * 128, :])
        emit_actquant_rows(kb, xrow[:], xq_bf[:], sinv_x[:, m:m + 1], qs, D)
        kb.dma("scalar", scr_xq[m * 128:(m + 1) * 128, :], xq_bf[:], ds=xq_ds)
    kb.barrier()
    sc.close()

    sc = ExitStack()
    emit_tern_group(kb, w_gate, scr_wgq, DFF, D, sc, "wg")
    emit_tern_group(kb, w_up, scr_wuq, DFF, D, sc, "wu")
    emit_tern_group(kb, w_down, scr_wdq, D, DFF, sc, "wd")
    kb.barrier()
    sc.close()

    sc = ExitStack()
    xqT = kb.sbuf("f0_xqT", [128, 16 * T], BF16, scope=sc)
    for k in range(16):
        kb.dma("sync", xqT[:, k * T:(k + 1) * T],
               scr_xq[:, k * 128:(k + 1) * 128], transpose=True)
    wgT = [kb.sbuf("f0_wgT%d" % i, [128, 16 * 512], BF16, scope=sc) for i in range(2)]
    wuT = [kb.sbuf("f0_wuT%d" % i, [128, 16 * 512], BF16, scope=sc) for i in range(2)]
    wg_ds = [kb.dsem("f0_wgds%d" % i, "sync") for i in range(2)]
    wu_ds = [kb.dsem("f0_wuds%d" % i, "scalar") for i in range(2)]
    sg_sb = kb.sbuf("f0_sg", [128, 512], F32, scope=sc)
    hid_sb = [kb.sbuf("f0_hid%d" % i, [128, 512], F32, scope=sc) for i in range(2)]
    hid_ds = [kb.dsem("f0_hidds%d" % i, "scalar") for i in range(2)]
    kb.barrier()
    psA = kb.psum("f0_psA", [128, 512], F32, scope=sc)
    psB = kb.psum("f0_psB", [128, 512], F32, scope=sc)
    slabs = [(i * 512, 512) for i in range(10)] + [(5120, 384)]
    for si, (s0, n) in enumerate(slabs):
        wgb, wub = wgT[si % 2], wuT[si % 2]
        for k in range(16):
            kb.dma("sync", wgb[:, k * 512:k * 512 + n],
                   scr_wgq[s0:s0 + n, k * 128:(k + 1) * 128], transpose=True,
                   ds=wg_ds[si % 2])
            kb.dma("scalar", wub[:, k * 512:k * 512 + n],
                   scr_wuq[s0:s0 + n, k * 128:(k + 1) * 128], transpose=True,
                   ds=wu_ds[si % 2])
        for m in range(16):
            for k in range(16):
                t.matmul(psA[:, 0:n], xqT[:, k * T + m * 128:k * T + m * 128 + 128],
                         wgb[:, k * 512:k * 512 + n],
                         start=(k == 0), stop=(k == 15), skip_group_check=True)
                t.matmul(psB[:, 0:n], xqT[:, k * T + m * 128:k * T + m * 128 + 128],
                         wub[:, k * 512:k * 512 + n],
                         start=(k == 0), stop=(k == 15), skip_group_check=True)
            s.activation(sg_sb[:, 0:n], psA[:, 0:n], AF.Silu,
                         scale=sinv_x[:, m:m + 1])
            hb = hid_sb[m % 2]
            v.scalar_tensor_tensor(hb[:, 0:n], psB[:, 0:n], sinv_x[:, m:m + 1],
                                   sg_sb[:, 0:n], ALU.mult, ALU.mult)
            kb.dma("scalar", scr_hid[m * 128:(m + 1) * 128, s0:s0 + n],
                   hb[:, 0:n], ds=hid_ds[m % 2])
    kb.barrier()
    sc.close()

    sc = ExitStack()
    hrow = kb.sbuf("f0_hrow", [128, DFF], F32, scope=sc)
    qs = {"t": kb.sbuf("f0_ht", [128, DFF], F32, scope=sc),
          "nf": kb.sbuf("f0_hn", [128, DFF], F32, scope=sc),
          "b1": kb.sbuf("f0_hb1", [128, DFF], F32, scope=sc),
          "b2": kb.sbuf("f0_hb2", [128, DFF], F32, scope=sc),
          "i32": kb.sbuf("f0_hi", [128, DFF], I32, scope=sc),
          "st": kb.sbuf("f0_hs", [128, 4], F32, scope=sc)}
    hq_bf = kb.sbuf("f0_hqb", [128, DFF], BF16, scope=sc)
    hq_ds = kb.dsem("f0_hqds", "scalar")
    for m in range(16):
        kb.dma("sync", hrow[:], scr_hid[m * 128:(m + 1) * 128, :])
        emit_actquant_rows(kb, hrow[:], hq_bf[:], sinv_h[:, m:m + 1], qs, DFF)
        kb.dma("scalar", scr_hq[m * 128:(m + 1) * 128, :], hq_bf[:], ds=hq_ds)
    kb.barrier()
    sc.close()

    KH = [(0, 15), (15, 14), (29, 14)]
    for half, (k0, nk) in enumerate(KH):
        sc = ExitStack()
        hqT = kb.sbuf("f0_hqT%d" % half, [128, nk * T], BF16, scope=sc)
        wdT = kb.sbuf("f0_wdT%d" % half, [128, nk * T], BF16, scope=sc)
        for k in range(nk):
            kb.dma("sync", hqT[:, k * T:(k + 1) * T],
                   scr_hq[:, (k0 + k) * 128:(k0 + k + 1) * 128], transpose=True)
            kb.dma("scalar", wdT[:, k * T:(k + 1) * T],
                   scr_wdq[:, (k0 + k) * 128:(k0 + k + 1) * 128],
                   transpose=True)
        kb.barrier()
        if DBG and half == 0:
            dbg_hqT = kb.dram("dbg_hqT", [128, T], BF16, kind="ExternalOutput")
            dbg_wdT = kb.dram("dbg_wdT", [128, T], BF16, kind="ExternalOutput")
            kb.dma("gpsimd", dbg_hqT, hqT[:, 0:T])
            kb.dma("gpsimd", dbg_wdT, wdT[:, 0:T])
        banks = [kb.psum("f0_pd%d_%d" % (half, i), [128, 512], F32, scope=sc)
                 for i in range(4)]
        prow = kb.sbuf("f0_prow%d" % half, [128, D], F32, scope=sc)
        orow = kb.sbuf("f0_orow%d" % half, [128, D], F32, scope=sc)
        x1r = kb.sbuf("f0_x1r%d" % half, [128, D], F32, scope=sc)
        lstat = kb.sbuf("f0_lst%d" % half, [128, 8], F32, scope=sc)
        pr_ds = kb.dsem("f0_prds%d" % half, "scalar")
        x2_ds = kb.dsem("f0_x2ds%d" % half, "gpsimd")
        st2 = MMStage(kb, banks, "f0dn%d" % half)
        for m in range(16):
            if half > 0:
                kb.dma("sync", prow[:], scr_pd[m * 128:(m + 1) * 128, :])
            if half == len(KH) - 1:
                kb.dma("sync", x1r[:], x1_in[m * 128:(m + 1) * 128, :])
            for k in range(nk):
                for n in range(4):
                    t.matmul(banks[n][:],
                             hqT[:, k * T + m * 128:k * T + m * 128 + 128],
                             wdT[:, k * T + n * 512:k * T + n * 512 + 512],
                             start=(k == 0), stop=(k == nk - 1),
                             skip_group_check=True)
            for n in range(4):
                if half == 0:
                    v.tensor_copy(orow[:, n * 512:(n + 1) * 512], banks[n][:])
                else:
                    v.tensor_tensor(orow[:, n * 512:(n + 1) * 512],
                                    banks[n][:], prow[:, n * 512:(n + 1) * 512],
                                    ALU.add)
            if half < len(KH) - 1:
                kb.dma("scalar", scr_pd[m * 128:(m + 1) * 128, :], orow[:],
                       ds=pr_ds)
            else:
                v.scalar_tensor_tensor(orow[:], orow[:], sinv_h[:, m:m + 1],
                                       x1r[:], ALU.mult, ALU.add)
                emit_ln_rows(kb, orow[:], orow[:], D, 1e-5, prow, lstat)
                kb.dma("gpsimd", x2_out[m * 128:(m + 1) * 128, :], orow[:],
                       ds=x2_ds)
        kb.barrier()
        sc.close()


def build_l1_attn(kb, x2_in, W, x3_out, cs):
    t, v, s, g = kb.t, kb.v, kb.s, kb.g
    SEG = 512
    NSEG = T // SEG
    ident = cs["ident"]
    ones_row = cs["ones_row"]
    cmq = cs["cmq"]
    cmk = cs["cmk"]
    dk = "ExternalOutput" if DBG else "Internal"
    scr_x2bf = kb.dram("a1_x2bf", [T, D], BF16, kind=dk)
    scr_q = kb.dram("a1_q", [T, D], F32, kind=dk)
    scr_qbf = kb.dram("a1_qbf", [T, D], BF16, kind=dk)
    scr_knf = kb.dram("a1_knf", [T, D], F32, kind=dk)
    scr_kn = kb.dram("a1_kn", [T, D], BF16, kind=dk)
    scr_ktil = kb.dram("a1_ktil", [T, D], BF16, kind=dk)
    scr_bmt = kb.dram("a1_bmt", [T, D], F32, kind=dk)
    scr_sg = kb.dram("a1_sg", [T, D], F32, kind=dk)
    scr_og = kb.dram("a1_og", [T, D], BF16, kind=dk)

    wT = kb.sbuf("a1_wT", [128, 16], F32)
    bT = kb.sbuf("a1_bT", [128, 16], F32)
    nwc = kb.sbuf("a1_nwc", [128, 16], F32)
    e_tm = kb.sbuf("a1_etm", [128, 16 * 16], F32)
    inv128c = kb.sbuf("a1_i128", [128, 1], F32)

    def etv(m):
        return e_tm[:, m * 16:(m + 1) * 16].unsqueeze(-1) \
            .broadcast_to([128, 16, 128])

    kb.dma("gpsimd", scr_x2bf, x2_in)
    lnw_r = kb.sbuf("a1_lnwr", [16, 128], F32)
    lnb_r = kb.sbuf("a1_lnbr", [16, 128], F32)
    nw_r = kb.sbuf("a1_nwr", [16, 128], F32)
    kb.dma("sync", lnw_r[:], W["ln_w"])
    kb.dma("sync", lnb_r[:], W["ln_b"])
    kb.dma("sync", nw_r[:], W["norm_w"].rearrange("o (a b) -> (o a) b", b=128))
    v.memset(inv128c[:], 1.0 / 128.0)
    kb.barrier()

    sc = ExitStack()
    ps0 = kb.psum("a1_ps0", [128, 512], F32, scope=sc)
    psT = kb.psum("a1_psT", [128, 128], F32, scope=sc)
    t.transpose(psT[:, 0:16], lnw_r[:], ident[0:16, 0:16])
    v.tensor_copy(wT[:], psT[:, 0:16])
    t.transpose(psT[:, 16:32], lnb_r[:], ident[0:16, 0:16])
    v.tensor_copy(bT[:], psT[:, 16:32])
    t.transpose(psT[:, 32:48], nw_r[:], ident[0:16, 0:16])
    v.tensor_copy(nwc[:], psT[:, 32:48])

    xT = kb.sbuf("a1_xT", [128, 16 * T], BF16, scope=sc)
    for k in range(16):
        kb.dma("sync", xT[:, k * T:(k + 1) * T],
               scr_x2bf[:, k * 128:(k + 1) * 128], transpose=True)
    wsm = kb.sbuf("a1_wsm", [128, 16 * 128], BF16, scope=sc)
    kb.dma("gpsimd", wsm[:, 0:16 * 16].rearrange("p (k n) -> p k n", k=16),
           W["wbeta"].rearrange("(k p) n -> p k n", p=128))
    kb.barrier()
    for m in range(16):
        for k in range(16):
            t.matmul(ps0[:, 0:16], xT[:, k * T + m * 128:k * T + m * 128 + 128],
                     wsm[:, k * 16:(k + 1) * 16], start=(k == 0), stop=(k == 15))
        s.activation(e_tm[:, m * 16:(m + 1) * 16], ps0[:, 0:16], AF.Sigmoid)
        v.tensor_scalar(e_tm[:, m * 16:(m + 1) * 16],
                        e_tm[:, m * 16:(m + 1) * 16], 2.0, None, ALU.mult)
    kb.dma("gpsimd", wsm[:].rearrange("p (k n) -> p k n", k=16),
           W["wg1"].rearrange("(k p) n -> p k n", p=128))
    g1T = kb.sbuf("a1_g1T", [128, T], BF16, scope=sc)
    g1f = kb.sbuf("a1_g1f", [128, 128], F32, scope=sc)
    for m in range(16):
        for k in range(16):
            t.matmul(ps0[:, 0:128], xT[:, k * T + m * 128:k * T + m * 128 + 128],
                     wsm[:, k * 128:(k + 1) * 128], start=(k == 0), stop=(k == 15))
        v.tensor_copy(g1f[:], ps0[:, 0:128])
        t.transpose(psT[:], g1f[:], ident[:])
        v.tensor_copy(g1T[:, m * 128:(m + 1) * 128], psT[:])
    kb.dma("gpsimd", wsm[:, 0:4 * 512].rearrange("p (k n) -> p k n", k=4),
           W["wg2"].rearrange("p (k n) -> p k n", k=4))
    sgrow = kb.sbuf("a1_sgrow", [128, D], F32, scope=sc)
    kb.barrier()
    for m in range(16):
        for n in range(4):
            t.matmul(ps0[:, 0:512], g1T[:, m * 128:(m + 1) * 128],
                     wsm[:, n * 512:(n + 1) * 512], start=True, stop=True)
            s.activation(sgrow[:, n * 512:(n + 1) * 512], ps0[:, 0:512],
                         AF.Sigmoid)
        kb.dma("scalar", scr_sg[m * 128:(m + 1) * 128, :], sgrow[:])
    kb.barrier()
    sc.close()

    sc = ExitStack()
    xT2 = kb.sbuf("a1_xT2", [128, 16 * T], BF16, scope=sc)
    for k in range(16):
        kb.dma("sync", xT2[:, k * T:(k + 1) * T],
               scr_x2bf[:, k * 128:(k + 1) * 128], transpose=True)
    wbig = kb.sbuf("a1_wbig", [128, 16 * T], BF16, scope=sc)
    krow = kb.sbuf("a1_krow", [128, D], F32, scope=sc)
    ksq = kb.sbuf("a1_ksq", [128, D], F32, scope=sc)
    kst = kb.sbuf("a1_kst", [128, 48], F32, scope=sc)
    kbf = kb.sbuf("a1_kbf", [128, D], BF16, scope=sc)
    brow = kb.sbuf("a1_brow", [128, D], F32, scope=sc)
    banks = [kb.psum("a1_pk%d" % i, [128, 512], F32, scope=sc) for i in range(4)]
    kb.dma("gpsimd", wbig[:].rearrange("p (k n) -> p k n", k=16),
           W["wk"].rearrange("(k p) n -> p k n", p=128))
    kb.barrier()

    def proj_all(m):
        for k in range(16):
            for n in range(4):
                t.matmul(banks[n][:],
                         xT2[:, k * T + m * 128:k * T + m * 128 + 128],
                         wbig[:, k * T + n * 512:k * T + n * 512 + 512],
                         start=(k == 0), stop=(k == 15), skip_group_check=True)

    for m in range(16):
        proj_all(m)
        for n in range(4):
            v.tensor_copy(krow[:, n * 512:(n + 1) * 512], banks[n][:])
        k3 = krow[:].rearrange("p (h e) -> p h e", h=16)
        s.activation(ksq[:], krow[:], AF.Square)
        v.tensor_reduce(kst[:, 0:16], ksq[:].rearrange("p (h e) -> p h e", h=16),
                        AX.X, ALU.add)
        s.activation(kst[:, 16:32], kst[:, 0:16], AF.Ln, bias=1e-6)
        s.activation(kst[:, 16:32], kst[:, 16:32], AF.Exp, scale=-0.5)
        v.tensor_tensor(k3, k3, kst[:, 16:32].unsqueeze(-1)
                        .broadcast_to([128, 16, 128]), ALU.mult)
        kb.dma("scalar", scr_knf[m * 128:(m + 1) * 128, :], krow[:])
        v.tensor_copy(kbf[:], krow[:])
        kb.dma("scalar", scr_kn[m * 128:(m + 1) * 128, :], kbf[:])
        v.tensor_tensor(kbf[:].rearrange("p (h e) -> p h e", h=16), k3,
                        etv(m), ALU.mult)
        kb.dma("scalar", scr_ktil[m * 128:(m + 1) * 128, :], kbf[:])
    kb.barrier()

    kb.dma("gpsimd", wbig[:].rearrange("p (k n) -> p k n", k=16),
           W["wv"].rearrange("(k p) n -> p k n", p=128))
    bflat = kb.sbuf("a1_bflat", [1, D], F32, scope=sc)
    kb.dma("sync", bflat[:], W["ln_b"].rearrange("a b -> (a b)").unsqueeze(0))
    b_bc = kb.sbuf("a1_bbc", [128, D], F32, scope=sc)
    kb.barrier()
    for n in range(4):
        t.matmul(banks[n][:], ones_row[:], bflat[:, n * 512:(n + 1) * 512],
                 start=True, stop=True)
        v.tensor_copy(b_bc[:, n * 512:(n + 1) * 512], banks[n][:])
    for m in range(16):
        kb.dma("sync", krow[:], scr_knf[m * 128:(m + 1) * 128, :])
        proj_all(m)
        for n in range(4):
            sl = slice(n * 512, (n + 1) * 512)
            v.tensor_tensor(brow[:, sl], b_bc[:, sl], banks[n][:], ALU.subtract)
            v.tensor_tensor(brow[:, sl], brow[:, sl], krow[:, sl], ALU.add)
        kb.dma("scalar", scr_bmt[m * 128:(m + 1) * 128, :], brow[:])
    kb.barrier()

    kb.dma("gpsimd", wbig[:].rearrange("p (k n) -> p k n", k=16),
           W["wq"].rearrange("(k p) n -> p k n", p=128))
    kb.barrier()
    for m in range(16):
        proj_all(m)
        for n in range(4):
            s.activation(krow[:, n * 512:(n + 1) * 512], banks[n][:], AF.Silu)
        kb.dma("scalar", scr_q[m * 128:(m + 1) * 128, :], krow[:])
    kb.barrier()
    sc.close()
    kb.dma("gpsimd", scr_qbf, scr_q)
    kb.barrier()

    sc = ExitStack()
    S_sb = kb.sbuf("a1_S", [128, 16 * 128], F32, scope=sc)
    S_bf = kb.sbuf("a1_Sbf", [128, 16 * 128], BF16, scope=sc)
    v.memset(S_sb[:], 0.0)
    v.memset(S_bf[:], 0.0)
    stag = kb.sbuf("a1_stag", [128, 512], F32, scope=sc)
    v.memset(stag[:], 0.0)
    ktp = kb.sbuf("a1_ktp", [128, D], BF16, scope=sc)
    v.memset(ktp[:], 0.0)
    ktilpad = kb.sbuf("a1_kpd", [128, 16 * 128], BF16, scope=sc)
    v.memset(ktilpad[:], 0.0)
    vstk = kb.sbuf("a1_vstk", [128, 16 * 128], BF16, scope=sc)
    v.memset(vstk[:], 0.0)
    khb_sb = kb.sbuf("a1_khb", [128, 16 * 64], F32, scope=sc)
    qhb_sb = kb.sbuf("a1_qhb", [128, 16 * 64], F32, scope=sc)
    cqT = kb.sbuf("a1_cqT", [128, 16 * 64], BF16, scope=sc)
    ckT = kb.sbuf("a1_ckT", [128, 16 * 64], BF16, scope=sc)
    kh_w = kb.sbuf("a1_khw", [128, 256], F32, scope=sc)
    khh = kb.sbuf("a1_khh", [128, 256], F32, scope=sc)
    gg_t = kb.sbuf("a1_gg", [128, 256], F32, scope=sc)
    gw_t = kb.sbuf("a1_gw", [128, 256], F32, scope=sc)
    pp_t = kb.sbuf("a1_pp", [128, 256], F32, scope=sc)
    t1_t = kb.sbuf("a1_t1", [128, 256], F32, scope=sc)
    t2_t = kb.sbuf("a1_t2", [128, 256], F32, scope=sc)
    sq_t = kb.sbuf("a1_sq", [128, 256], F32, scope=sc)
    rows = kb.sbuf("a1_rows", [1, 512], F32, scope=sc)
    rows2 = kb.sbuf("a1_rows2", [1, 512], F32, scope=sc)
    bmt_em = kb.sbuf("a1_bmem", [128, 16 * SEG], F32, scope=sc)
    doT = kb.sbuf("a1_doT", [128, 16 * SEG], F32, scope=sc)
    knT_s = kb.sbuf("a1_knT", [128, 16 * SEG], BF16, scope=sc)
    ktT_s = kb.sbuf("a1_ktT", [128, 16 * SEG], BF16, scope=sc)
    qT_s = kb.sbuf("a1_qT", [128, 16 * SEG], BF16, scope=sc)
    brow2 = kb.sbuf("a1_brow2", [128, D], F32, scope=sc)
    wv3 = wT[:].unsqueeze(-1).broadcast_to([128, 16, 16])
    bv3 = bT[:].unsqueeze(-1).broadcast_to([128, 16, 16])

    psK = kb.psum("a1_psK", [128, 512], F32, scope=sc)
    psK2 = kb.psum("a1_psK2", [128, 512], F32, scope=sc)
    psQ = kb.psum("a1_psQ", [128, 512], F32, scope=sc)
    psQ2 = kb.psum("a1_psQ2", [128, 512], F32, scope=sc)
    psC = kb.psum("a1_psC", [128, 512], F32, scope=sc)
    psR = kb.psum("a1_psR", [1, 512], F32, scope=sc)
    psB = kb.psum("a1_psB", [128, 512], F32, scope=sc)
    psX = kb.psum("a1_psX", [128, 512], F32, scope=sc)

    def seg_chunk_view(buf, h_stride, tok_off, width):
        return buf[:].rearrange("p (h s) -> p h s", h=16)[:, :,
                                                          tok_off:tok_off + width]

    for seg in range(NSEG):
        t0s = seg * SEG
        for k in range(16):
            kb.dma("sync", knT_s[:, k * SEG:(k + 1) * SEG],
                   scr_kn[t0s:t0s + SEG, k * 128:(k + 1) * 128], transpose=True)
            kb.dma("sync", ktT_s[:, k * SEG:(k + 1) * SEG],
                   scr_ktil[t0s:t0s + SEG, k * 128:(k + 1) * 128], transpose=True)
            kb.dma("sync", qT_s[:, k * SEG:(k + 1) * SEG],
                   scr_qbf[t0s:t0s + SEG, k * 128:(k + 1) * 128], transpose=True)
        for mt in range(SEG // 128):
            kb.dma("sync", brow2[:], scr_bmt[t0s + mt * 128:t0s + mt * 128 + 128, :])
            for h in range(16):
                t.transpose(psX[:, 0:128], brow2[:, h * 128:(h + 1) * 128],
                            ident[:])
                v.tensor_copy(bmt_em[:, h * SEG + mt * 128:
                                     h * SEG + mt * 128 + 128], psX[:, 0:128])
        for grp in range(SEG // 64):
            gt0 = grp * 64
            for c in range(4):
                kb.dma("sync", ktp[32 * c:32 * c + 16, :],
                       scr_ktil[t0s + gt0 + 16 * c:t0s + gt0 + 16 * c + 16, :])
            for h in range(16):
                v.tensor_copy(
                    ktilpad[:, h * 128:h * 128 + 128]
                    .rearrange("p (c s) -> p c s", c=4)[:, :, 0:16],
                    ktT_s[:, h * SEG + gt0:h * SEG + gt0 + 64]
                    .rearrange("p (c s) -> p c s", c=4))
                t.matmul(psC[:, 0:64], ktilpad[:, h * 128:h * 128 + 128],
                         qT_s[:, h * SEG + gt0:h * SEG + gt0 + 64],
                         start=True, stop=True)
                v.tensor_tensor(cqT[:, h * 64:(h + 1) * 64], psC[:, 0:64],
                                cmq[:], ALU.mult)
                t.matmul(psC[:, 0:64], ktilpad[:, h * 128:h * 128 + 128],
                         knT_s[:, h * SEG + gt0:h * SEG + gt0 + 64],
                         start=True, stop=True)
                v.tensor_tensor(ckT[:, h * 64:(h + 1) * 64], psC[:, 0:64],
                                cmk[:], ALU.mult)
                pk = psK if h < 8 else psK2
                pq = psQ if h < 8 else psQ2
                ho = (h % 8) * 64
                t.matmul(pk[:, ho:ho + 64], S_bf[:, h * 128:(h + 1) * 128],
                         knT_s[:, h * SEG + gt0:h * SEG + gt0 + 64],
                         start=True, stop=True)
                t.matmul(pq[:, ho:ho + 64], S_bf[:, h * 128:(h + 1) * 128],
                         qT_s[:, h * SEG + gt0:h * SEG + gt0 + 64],
                         start=True, stop=True)
            v.tensor_copy(khb_sb[:, 0:512], psK[:])
            v.tensor_copy(khb_sb[:, 512:1024], psK2[:])
            v.tensor_copy(qhb_sb[:, 0:512], psQ[:])
            v.tensor_copy(qhb_sb[:, 512:1024], psQ2[:])

            for c in range(4):
                co = 16 * c
                khbv = khb_sb[:].rearrange("p (h s) -> p h s", h=16)[:, :, co:co + 16]
                qhbv = qhb_sb[:].rearrange("p (h s) -> p h s", h=16)[:, :, co:co + 16]
                kh3 = kh_w[:].rearrange("p (h s) -> p h s", h=16)
                for h in range(16):
                    t.matmul(psC[:, h * 16:h * 16 + 16],
                             vstk[:, h * 128:(h + 1) * 128],
                             ckT[:, h * 64 + co:h * 64 + co + 16],
                             start=True, stop=True, skip_group_check=True)
                v.tensor_tensor(kh3, khbv,
                                psC[:, 0:256].rearrange("p (h s) -> p h s", h=16),
                                ALU.subtract)
                t.matmul(psR[:, 0:256], inv128c[:], kh_w[:], start=True, stop=True)
                s.activation(sq_t[:], kh_w[:], AF.Square)
                t.matmul(psR[:, 256:512], inv128c[:], sq_t[:], start=True, stop=True)
                v.tensor_copy(rows[:, 0:256], psR[:, 0:256])
                v.tensor_tensor(rows[:, 256:512], rows[:, 0:256], rows[:, 0:256],
                                ALU.mult)
                v.tensor_tensor(rows[:, 256:512], psR[:, 256:512], rows[:, 256:512],
                                ALU.subtract)
                s.activation(rows[:, 256:512], rows[:, 256:512], AF.Ln, bias=1e-6)
                s.activation(rows[:, 256:512], rows[:, 256:512], AF.Exp, scale=-0.5)
                t.matmul(psB[:, 0:256], ones_row[:], rows[:, 0:256],
                         start=True, stop=True)
                t.matmul(psB[:, 256:512], ones_row[:], rows[:, 256:512],
                         start=True, stop=True)
                v.tensor_tensor(khh[:], kh_w[:], psB[:, 0:256], ALU.subtract)
                v.tensor_tensor(khh[:], khh[:], psB[:, 256:512], ALU.mult)
                kh3h = khh[:].rearrange("p (h s) -> p h s", h=16)
                g3 = gg_t[:].rearrange("p (h s) -> p h s", h=16)
                v.tensor_tensor(g3, kh3h, wv3, ALU.mult)
                v.tensor_tensor(g3, g3,
                                seg_chunk_view(bmt_em, SEG, gt0 + co, 16),
                                ALU.add)
                v.tensor_tensor(gw_t[:].rearrange("p (h s) -> p h s", h=16),
                                g3, wv3, ALU.mult)
                v.tensor_tensor(pp_t[:], gw_t[:], khh[:], ALU.mult)
                t.matmul(psR[:, 0:256], inv128c[:], gw_t[:], start=True, stop=True)
                t.matmul(psR[:, 256:512], inv128c[:], pp_t[:], start=True, stop=True)
                v.tensor_copy(rows2[:], psR[:])
                t.matmul(psB[:, 0:256], ones_row[:], rows2[:, 0:256],
                         start=True, stop=True, skip_group_check=True)
                t.matmul(psB[:, 256:512], ones_row[:], rows2[:, 256:512],
                         start=True, stop=True, skip_group_check=True)
                v.tensor_tensor(t1_t[:], gw_t[:], psB[:, 0:256], ALU.subtract)
                v.tensor_tensor(t2_t[:], khh[:], psB[:, 256:512], ALU.mult)
                v.tensor_tensor(t1_t[:], t1_t[:], t2_t[:], ALU.subtract)
                t.matmul(psB[:, 0:256], ones_row[:], rows[:, 256:512],
                         start=True, stop=True, skip_group_check=True)
                v.tensor_tensor(t1_t[:], t1_t[:], psB[:, 0:256], ALU.mult)
                v.tensor_copy(stag[:].rearrange("p (h s) -> p h s", h=16)[:, :, 0:16],
                              t1_t[:].rearrange("p (h s) -> p h s", h=16))
                for tq in range(4):
                    t.transpose(psX[:, tq * 128:(tq + 1) * 128],
                                stag[:, tq * 128:(tq + 1) * 128], ident[:])
                for h in range(16):
                    tq = h // 4
                    ho = 32 * (h % 4)
                    v.tensor_copy(vstk[32 * c:32 * c + 32, h * 128:(h + 1) * 128],
                                  psX[ho:ho + 32, tq * 128:(tq + 1) * 128])
                for h in range(16):
                    t.matmul(psC[:, 256 + h * 16:256 + h * 16 + 16],
                             vstk[:, h * 128:(h + 1) * 128],
                             cqT[:, h * 64 + co:h * 64 + co + 16],
                             start=True, stop=True, skip_group_check=True)
                v.tensor_tensor(kh_w[:].rearrange("p (h s) -> p h s", h=16), qhbv,
                                psC[:, 256:512].rearrange("p (h s) -> p h s", h=16),
                                ALU.subtract)
                t.matmul(psR[:, 0:256], inv128c[:], kh_w[:], start=True, stop=True,
                         skip_group_check=True)
                s.activation(sq_t[:], kh_w[:], AF.Square)
                t.matmul(psR[:, 256:512], inv128c[:], sq_t[:], start=True,
                         stop=True, skip_group_check=True)
                v.tensor_copy(rows[:, 0:256], psR[:, 0:256])
                v.tensor_tensor(rows[:, 256:512], rows[:, 0:256], rows[:, 0:256],
                                ALU.mult)
                v.tensor_tensor(rows[:, 256:512], psR[:, 256:512], rows[:, 256:512],
                                ALU.subtract)
                s.activation(rows[:, 256:512], rows[:, 256:512], AF.Ln, bias=1e-6)
                s.activation(rows[:, 256:512], rows[:, 256:512], AF.Exp, scale=-0.5)
                t.matmul(psB[:, 0:256], ones_row[:], rows[:, 0:256],
                         start=True, stop=True, skip_group_check=True)
                t.matmul(psB[:, 256:512], ones_row[:], rows[:, 256:512],
                         start=True, stop=True, skip_group_check=True)
                v.tensor_tensor(khh[:], kh_w[:], psB[:, 0:256], ALU.subtract)
                v.tensor_tensor(khh[:], khh[:], psB[:, 256:512], ALU.mult)
                do3 = doT[:].rearrange("p (h s) -> p h s", h=16)[:, :,
                                                                gt0 + co:gt0 + co + 16]
                v.tensor_tensor(t2_t[:].rearrange("p (h s) -> p h s", h=16),
                                khh[:].rearrange("p (h s) -> p h s", h=16),
                                wv3, ALU.mult)
                v.tensor_tensor(do3,
                                t2_t[:].rearrange("p (h s) -> p h s", h=16),
                                bv3, ALU.add)
            for h in range(16):
                t.matmul(psX[:, 128 * (h % 4):128 * (h % 4) + 128],
                         ktp[:, h * 128:(h + 1) * 128],
                         vstk[:, h * 128:(h + 1) * 128],
                         start=True, stop=True, skip_group_check=True)
                if h % 4 == 3:
                    for hh in range(h - 3, h + 1):
                        v.tensor_tensor(S_sb[:, hh * 128:(hh + 1) * 128],
                                        S_sb[:, hh * 128:(hh + 1) * 128],
                                        psX[:, 128 * (hh % 4):128 * (hh % 4) + 128],
                                        ALU.subtract)
                        v.tensor_copy(S_bf[:, hh * 128:(hh + 1) * 128],
                                      S_sb[:, hh * 128:(hh + 1) * 128])
        dorow = kb.sbuf("a1_dor%d" % seg, [128, D], F32, scope=sc) \
            if seg == 0 else dorow
        qrow = kb.sbuf("a1_qr%d" % seg, [128, D], F32, scope=sc) \
            if seg == 0 else qrow
        ogst = kb.sbuf("a1_ogs%d" % seg, [128, 40], F32, scope=sc) \
            if seg == 0 else ogst
        ogbf = kb.sbuf("a1_ogb%d" % seg, [128, D], BF16, scope=sc) \
            if seg == 0 else ogbf
        for mt in range(SEG // 128):
            gtok = t0s + mt * 128
            for h in range(16):
                t.transpose(psX[:, 0:128],
                            doT[:, h * SEG + mt * 128:h * SEG + mt * 128 + 128],
                            ident[:])
                v.tensor_copy(dorow[:, h * 128:(h + 1) * 128], psX[:, 0:128])
            kb.dma("sync", qrow[:], scr_q[gtok:gtok + 128, :])
            kb.dma("sync", brow2[:], scr_sg[gtok:gtok + 128, :])
            v.tensor_tensor(dorow[:], dorow[:], qrow[:], ALU.add)
            v.tensor_tensor(dorow[:], dorow[:], brow2[:], ALU.mult)
            s.activation(qrow[:], dorow[:], AF.Square)
            v.tensor_reduce(ogst[:, 0:16],
                            qrow[:].rearrange("p (h e) -> p h e", h=16),
                            AX.X, ALU.add)
            s.activation(ogst[:, 16:32], ogst[:, 0:16], AF.Ln, bias=1e-6,
                         scale=1.0 / 128)
            s.activation(ogst[:, 16:32], ogst[:, 16:32], AF.Exp, scale=-0.5)
            v.tensor_tensor(ogbf[:].rearrange("p (h e) -> p h e", h=16),
                            dorow[:].rearrange("p (h e) -> p h e", h=16),
                            ogst[:, 16:32].unsqueeze(-1)
                            .broadcast_to([128, 16, 128]), ALU.mult)
            kb.dma("scalar", scr_og[gtok:gtok + 128, :], ogbf[:])
    kb.barrier()
    sc.close()

    sc = ExitStack()
    ogT = kb.sbuf("a1_ogT", [128, 16 * T], BF16, scope=sc)
    for k in range(16):
        kb.dma("sync", ogT[:, k * T:(k + 1) * T],
               scr_og[:, k * 128:(k + 1) * 128], transpose=True)
    wo_sb = kb.sbuf("a1_wo", [128, 16 * T], BF16, scope=sc)
    worow = kb.sbuf("a1_worow", [128, D], F32, scope=sc)
    for k in range(16):
        kb.dma("sync", worow[:], W["wo"][k * 128:(k + 1) * 128, :])
        v.tensor_scalar(wo_sb[:, k * T:(k + 1) * T], worow[:], nwc[:, k:k + 1],
                        None, ALU.mult)
    xres = kb.sbuf("a1_xres", [128, D], F32, scope=sc)
    orow = kb.sbuf("a1_orow", [128, D], F32, scope=sc)
    lstat = kb.sbuf("a1_lst", [128, 8], F32, scope=sc)
    ltmp = kb.sbuf("a1_ltmp", [128, D], F32, scope=sc)
    kb.barrier()
    banks = [kb.psum("a1_po%d" % i, [128, 512], F32, scope=sc) for i in range(4)]
    for m in range(16):
        kb.dma("sync", xres[:], x2_in[m * 128:(m + 1) * 128, :])
        for k in range(16):
            for n in range(4):
                t.matmul(banks[n][:],
                         ogT[:, k * T + m * 128:k * T + m * 128 + 128],
                         wo_sb[:, k * T + n * 512:k * T + n * 512 + 512],
                         start=(k == 0), stop=(k == 15), skip_group_check=True)
        for n in range(4):
            v.tensor_tensor(orow[:, n * 512:(n + 1) * 512], banks[n][:],
                            xres[:, n * 512:(n + 1) * 512], ALU.add)
        emit_ln_rows(kb, orow[:], orow[:], D, 1e-5, ltmp, lstat)
        kb.dma("gpsimd", x3_out[m * 128:(m + 1) * 128, :], orow[:])
    kb.barrier()
    sc.close()


def emit_tern_global(kb, w_dram, out_dram, O, I, sc, tag, gcol, eps=1e-6):
    t, v, s = kb.t, kb.v, kb.s
    ones_row = kb.cs["ones_row"]
    wt = kb.sbuf(f"gw_{tag}", [128, I], F32, scope=sc)
    acc = kb.sbuf(f"ga_{tag}", [128, 2], F32, scope=sc)
    tq = kb.sbuf(f"gt_{tag}", [128, I], F32, scope=sc)
    wq = kb.sbuf(f"gq_{tag}", [128, I], BF16, scope=sc)
    pgs = kb.psum(f"gp_{tag}", [128, 8], F32, scope=sc)
    nrt = O // 128
    for rt in range(nrt):
        kb.dma("sync", wt[:], w_dram[rt * 128:(rt + 1) * 128, :])
        v.tensor_reduce(acc[:, 1:2], wt[:], AX.X, ALU.add,
                        apply_absolute_value=True)
        if rt == 0:
            v.tensor_copy(acc[:, 0:1], acc[:, 1:2])
        else:
            v.tensor_tensor(acc[:, 0:1], acc[:, 0:1], acc[:, 1:2], ALU.add)
    t.matmul(pgs[0:1, 0:1], kb.cs["ones_col"][:], acc[:, 0:1],
             start=True, stop=True)
    s.activation(acc[0:1, 1:2], pgs[0:1, 0:1], AF.Copy, scale=1.0 / (O * I))
    t.matmul(pgs[:, 0:1], ones_row[:], acc[0:1, 1:2], start=True, stop=True)
    v.tensor_copy(gcol[:, 0:1], pgs[:, 0:1])
    v.tensor_scalar(gcol[:, 1:2], gcol[:, 0:1], float(eps), 0.5,
                    ALU.add, ALU.mult)
    for rt in range(nrt):
        kb.dma("sync", wt[:], w_dram[rt * 128:(rt + 1) * 128, :])
        v.tensor_scalar(tq[:], wt[:], gcol[:, 1:2], None, ALU.is_ge)
        v.tensor_scalar(wt[:], wt[:], -1.0, gcol[:, 1:2], ALU.mult, ALU.is_ge)
        v.scalar_tensor_tensor(wq[:], wt[:], -1.0, tq[:], ALU.mult, ALU.add)
        kb.dma("scalar", out_dram[rt * 128:(rt + 1) * 128, :], wq[:])


def emit_absmax_quant(kb, xn_ap, out_bf_ap, sc_col, st):
    v = kb.v
    gam, sm = st[:, 0:1], st[:, 1:2]
    v.tensor_reduce(gam, xn_ap, AX.X, ALU.max, apply_absolute_value=True)
    v.reciprocal(sm, gam)
    v.tensor_scalar(sm, sm, 128.0, None, ALU.mult)
    v.tensor_scalar(sc_col, gam, 1.0 / 128.0, None, ALU.mult)
    v.tensor_scalar(out_bf_ap, xn_ap, sm, 128.0, ALU.mult, ALU.min)
    return v.tensor_scalar(out_bf_ap, out_bf_ap, -128.0, None, ALU.max)


def build_l1_ffn(kb, x3_in, w_fc, w_proj, out_dram, cs):
    t, v, s, g = kb.t, kb.v, kb.s, kb.g
    DF = 8192
    dk = "ExternalOutput" if DBG else "Internal"
    scr_xq = kb.dram("f1_xq", [T, D], BF16, kind=dk)
    scr_wfq = kb.dram("f1_wfq", [DF, D], BF16, kind=dk)
    scr_wpq = kb.dram("f1_wpq", [D, DF], BF16, kind=dk)
    scr_h = kb.dram("f1_h", [T, DF], F32, kind=dk)
    scr_hq = kb.dram("f1_hq", [T, DF], BF16, kind=dk)
    scr_pd = kb.dram("f1_pd", [T, D], F32)
    sc1 = kb.sbuf("f1_sc1", [128, 16], F32)
    sc2 = kb.sbuf("f1_sc2", [128, 16], F32)
    gwf = kb.sbuf("f1_gwf", [128, 2], F32)
    gwp = kb.sbuf("f1_gwp", [128, 2], F32)

    sc = ExitStack()
    xrow = kb.sbuf("f1_xrow", [128, D], F32, scope=sc)
    ltmp = kb.sbuf("f1_lt", [128, D], F32, scope=sc)
    lstat = kb.sbuf("f1_ls", [128, 8], F32, scope=sc)
    xq_bf = kb.sbuf("f1_xqb", [128, D], BF16, scope=sc)
    for m in range(16):
        kb.dma("sync", xrow[:], x3_in[m * 128:(m + 1) * 128, :])
        emit_ln_rows(kb, xrow[:], ltmp[:, 0:D], D, 1e-6, ltmp, lstat)
        emit_absmax_quant(kb, ltmp[:, 0:D], xq_bf[:], sc1[:, m:m + 1], lstat)
        kb.dma("scalar", scr_xq[m * 128:(m + 1) * 128, :], xq_bf[:])
    kb.barrier()
    sc.close()

    sc = ExitStack()
    emit_tern_global(kb, w_fc, scr_wfq, DF, D, sc, "wf", gwf)
    kb.barrier()
    sc.close()
    sc = ExitStack()
    emit_tern_global(kb, w_proj, scr_wpq, D, DF, sc, "wp", gwp)
    kb.v.tensor_scalar(sc1[:], sc1[:], gwf[:, 0:1], None, ALU.mult)
    kb.barrier()
    sc.close()

    sc = ExitStack()
    xqT = kb.sbuf("f1_xqT", [128, 16 * T], BF16, scope=sc)
    for k in range(16):
        kb.dma("sync", xqT[:, k * T:(k + 1) * T],
               scr_xq[:, k * 128:(k + 1) * 128], transpose=True)
    wfT = [kb.sbuf("f1_wfT%d" % i, [128, 16 * 512], BF16, scope=sc)
           for i in range(4)]
    hrow = [kb.sbuf("f1_hr%d" % i, [128, 512], F32, scope=sc) for i in range(2)]
    kb.barrier()
    psC = [kb.psum("f1_psC%d" % i, [128, 512], F32, scope=sc) for i in range(4)]
    for sg4 in range(4):
        for j in range(4):
            slab = sg4 * 4 + j
            for k in range(16):
                kb.dma("sync", wfT[j][:, k * 512:(k + 1) * 512],
                       scr_wfq[slab * 512:slab * 512 + 512,
                               k * 128:(k + 1) * 128], transpose=True)
        for m in range(16):
            for k in range(16):
                for j in range(4):
                    t.matmul(psC[j][:],
                             xqT[:, k * T + m * 128:k * T + m * 128 + 128],
                             wfT[j][:, k * 512:(k + 1) * 512],
                             start=(k == 0), stop=(k == 15),
                             skip_group_check=True)
            for j in range(4):
                hb = hrow[j % 2]
                s.activation(hb[:], psC[j][:], AF.Gelu, scale=sc1[:, m:m + 1])
                kb.dma("scalar", scr_h[m * 128:(m + 1) * 128,
                                       (sg4 * 4 + j) * 512:(sg4 * 4 + j + 1) * 512],
                       hb[:])
    kb.barrier()
    sc.close()

    sc = ExitStack()
    hrow2 = kb.sbuf("f1_h2", [128, DF], F32, scope=sc)
    ltmp2 = kb.sbuf("f1_lt2", [128, DF], F32, scope=sc)
    lstat2 = kb.sbuf("f1_ls2", [128, 8], F32, scope=sc)
    hq_bf = kb.sbuf("f1_hqb", [128, DF], BF16, scope=sc)
    for m in range(16):
        kb.dma("sync", hrow2[:], scr_h[m * 128:(m + 1) * 128, :])
        emit_ln_rows(kb, hrow2[:], ltmp2[:, 0:DF], DF, 1e-6, ltmp2, lstat2)
        emit_absmax_quant(kb, ltmp2[:, 0:DF], hq_bf[:], sc2[:, m:m + 1], lstat2)
        kb.dma("scalar", scr_hq[m * 128:(m + 1) * 128, :], hq_bf[:])
    kb.v.tensor_scalar(sc2[:], sc2[:], gwp[:, 0:1], None, ALU.mult)
    kb.barrier()
    sc.close()

    KH = [(0, 16), (16, 16), (32, 16), (48, 16)]
    for half, (k0, nk) in enumerate(KH):
        sc = ExitStack()
        hqT = kb.sbuf("f1_hqT%d" % half, [128, nk * T], BF16, scope=sc)
        wpT = kb.sbuf("f1_wpT%d" % half, [128, nk * T], BF16, scope=sc)
        for k in range(nk):
            kb.dma("sync", hqT[:, k * T:(k + 1) * T],
                   scr_hq[:, (k0 + k) * 128:(k0 + k + 1) * 128], transpose=True)
            kb.dma("sync", wpT[:, k * T:(k + 1) * T],
                   scr_wpq[:, (k0 + k) * 128:(k0 + k + 1) * 128], transpose=True)
        kb.barrier()
        banks = [kb.psum("f1_pd%d_%d" % (half, i), [128, 512], F32, scope=sc)
                 for i in range(4)]
        prow = kb.sbuf("f1_prow%d" % half, [128, D], F32, scope=sc)
        orow = kb.sbuf("f1_orow%d" % half, [128, D], F32, scope=sc)
        x3r = kb.sbuf("f1_x3r%d" % half, [128, D], F32, scope=sc)
        lstat3 = kb.sbuf("f1_ls3%d" % half, [128, 8], F32, scope=sc)
        st2 = MMStage(kb, banks, "f1dn%d" % half)
        for m in range(16):
            if half > 0:
                kb.dma("sync", prow[:], scr_pd[m * 128:(m + 1) * 128, :])
            if half == len(KH) - 1:
                kb.dma("sync", x3r[:], x3_in[m * 128:(m + 1) * 128, :])
            for k in range(nk):
                for n in range(4):
                    t.matmul(banks[n][:],
                             hqT[:, k * T + m * 128:k * T + m * 128 + 128],
                             wpT[:, k * T + n * 512:k * T + n * 512 + 512],
                             start=(k == 0), stop=(k == nk - 1),
                             skip_group_check=True)
            for n in range(4):
                if half == 0:
                    v.tensor_copy(orow[:, n * 512:(n + 1) * 512], banks[n][:])
                else:
                    v.tensor_tensor(orow[:, n * 512:(n + 1) * 512],
                                    banks[n][:], prow[:, n * 512:(n + 1) * 512],
                                    ALU.add)
            if half < len(KH) - 1:
                kb.dma("scalar", scr_pd[m * 128:(m + 1) * 128, :], orow[:])
            else:
                v.scalar_tensor_tensor(orow[:], orow[:], sc2[:, m:m + 1],
                                       x3r[:], ALU.mult, ALU.add)
                emit_ln_rows(kb, orow[:], orow[:], D, 1e-5, prow, lstat3)
                kb.dma("gpsimd", out_dram[m * 128:(m + 1) * 128, :], orow[:])
        kb.barrier()
        sc.close()


def make_consts():
    ident = np.eye(128, dtype=np.float32)
    p = np.arange(128)[:, None]
    f = np.arange(512)[None, :]
    masks = [np.where(p + 128 * j > f, NEGBIG, 0.0) for j in range(4)]
    dmask = np.concatenate(masks, axis=1).astype(np.float32)
    eh2 = np.zeros((128, 2), dtype=np.float32)
    eh2[0:64, 0] = 1.0
    eh2[64:128, 1] = 1.0
    selrow = np.zeros((2, 256), dtype=np.float32)
    selrow[0, 0:128] = 1.0
    selrow[1, 128:256] = 1.0
    js = np.arange(128)
    ic = np.arange(64)
    js_tok = 16 * (js // 32) + (js % 32)
    js_real = (js % 32) < 16
    cmq = (js_real[:, None] & (js_tok[:, None] <= ic[None, :])).astype(np.float32)
    cmk = (js_real[:, None] & ((js // 32)[:, None] * 16 + 16 <= ic[None, :])
           ).astype(np.float32)
    return {"c_ident": ident, "c_dmask": dmask, "c_eh2": eh2,
            "c_selrow": selrow, "c_cmq": cmq, "c_cmk": cmk}


def load_consts(kb):
    cs = {}
    dmask_d = kb.dram("c_dmask", [128, 2048], F32, kind="ExternalInput")
    eh2_d = kb.dram("c_eh2", [128, 2], F32, kind="ExternalInput")
    selrow_d = kb.dram("c_selrow", [2, 256], F32, kind="ExternalInput")
    ident_d = kb.dram("c_ident", [128, 128], F32, kind="ExternalInput")
    cs["dmask"] = kb.sbuf("sb_dmask", [128, 2048], F32)
    cs["eh2"] = kb.sbuf("sb_eh2", [128, 2], F32)
    cs["selrow"] = kb.sbuf("sb_selrow", [2, 256], F32)
    cs["ident"] = kb.sbuf("sb_ident", [128, 128], F32)
    kb.dma("sync", cs["dmask"][:], dmask_d)
    kb.dma("sync", cs["eh2"][:], eh2_d)
    kb.dma("sync", cs["selrow"][:], selrow_d)
    kb.dma("sync", cs["ident"][:], ident_d)
    cmq_d = kb.dram("c_cmq", [128, 64], F32, kind="ExternalInput")
    cmk_d = kb.dram("c_cmk", [128, 64], F32, kind="ExternalInput")
    cs["cmq"] = kb.sbuf("sb_cmq", [128, 64], F32)
    cs["cmk"] = kb.sbuf("sb_cmk", [128, 64], F32)
    kb.dma("sync", cs["cmq"][:], cmq_d)
    kb.dma("sync", cs["cmk"][:], cmk_d)
    cs["ones_col"] = kb.sbuf("sb_ones_col", [128, 1], F32)
    cs["ones_colb"] = kb.sbuf("sb_ones_colb", [128, 1], BF16)
    cs["ones_row"] = kb.sbuf("sb_ones_row", [1, 128], F32)
    kb.v.memset(cs["ones_col"][:], 1.0)
    kb.v.memset(cs["ones_colb"][:], 1.0)
    kb.v.memset(cs["ones_row"][:], 1.0)
    pool = kb.sbuf("sb_constf", [128, 16], F32)
    for i, val in enumerate([1e-5, 1e-6, 0.5, 1.0]):
        kb.v.memset(pool[:, i:i + 1], float(val))
        kb.nc.const_aps.register(F32, float(val), pool[:, i:i + 1])
    return cs


def build_l0a_program():
    nc = bass.Bass("TRN2", target_bir_lowering=False)
    kb = KB(nc)
    x_in = kb.dram("x_in", [T, D], F32, kind="ExternalInput")
    wqkv = kb.dram("wqkv", [D, 3 * D], F32, kind="ExternalInput")
    wproj = kb.dram("wproj", [D, D], F32, kind="ExternalInput")
    lam_in = kb.dram("lam_in", [1, 256], F32, kind="ExternalInput")
    x1_out = kb.dram("x1_out", [T, D], F32, kind="ExternalOutput")
    cs = load_consts(kb)
    kb.cs = cs
    kb.barrier()
    build_l0_attn(kb, x_in, wqkv, wproj, lam_in, x1_out, cs)
    return kb.finalize()


def build_l0f_program():
    nc = bass.Bass("TRN2", target_bir_lowering=False)
    kb = KB(nc)
    x1_in = kb.dram("x1_in", [T, D], F32, kind="ExternalInput")
    w_gate = kb.dram("w_gate", [5504, D], F32, kind="ExternalInput")
    w_up = kb.dram("w_up", [5504, D], F32, kind="ExternalInput")
    w_down = kb.dram("w_down", [D, 5504], F32, kind="ExternalInput")
    x2_out = kb.dram("x2_out", [T, D], F32, kind="ExternalOutput")
    cs = load_consts(kb)
    kb.cs = cs
    kb.barrier()
    build_l0_ffn(kb, x1_in, w_gate, w_up, w_down, x2_out, cs)
    return kb.finalize()


def build_l1f_program():
    nc = bass.Bass("TRN2", target_bir_lowering=False)
    kb = KB(nc)
    x3_in = kb.dram("x3_in", [T, D], F32, kind="ExternalInput")
    w_fc = kb.dram("w_fc", [8192, D], F32, kind="ExternalInput")
    w_proj = kb.dram("w_proj", [D, 8192], F32, kind="ExternalInput")
    out_d = kb.dram("out_d", [T, D], F32, kind="ExternalOutput")
    kb.cs = load_consts(kb)
    kb.barrier()
    build_l1_ffn(kb, x3_in, w_fc, w_proj, out_d, kb.cs)
    return kb.finalize()


def build_l1a_program():
    nc = bass.Bass("TRN2", target_bir_lowering=False)
    kb = KB(nc)
    x2_in = kb.dram("x2_in", [T, D], F32, kind="ExternalInput")
    W = {"wq": kb.dram("wq", [D, D], F32, kind="ExternalInput"),
         "wk": kb.dram("wk", [D, D], F32, kind="ExternalInput"),
         "wv": kb.dram("wv", [D, D], F32, kind="ExternalInput"),
         "wo": kb.dram("wo", [D, D], F32, kind="ExternalInput"),
         "wbeta": kb.dram("wbeta", [D, 16], F32, kind="ExternalInput"),
         "ln_w": kb.dram("ln_w", [16, 128], F32, kind="ExternalInput"),
         "ln_b": kb.dram("ln_b", [16, 128], F32, kind="ExternalInput"),
         "wg1": kb.dram("wg1", [D, 128], F32, kind="ExternalInput"),
         "wg2": kb.dram("wg2", [128, D], F32, kind="ExternalInput"),
         "norm_w": kb.dram("norm_w", [1, D], F32, kind="ExternalInput")}
    x3_out = kb.dram("x3_out", [T, D], F32, kind="ExternalOutput")
    kb.cs = load_consts(kb)
    kb.barrier()
    build_l1_attn(kb, x2_in, W, x3_out, kb.cs)
    return kb.finalize()


def build_fused_program():
    nc = bass.Bass("TRN2", target_bir_lowering=False)
    kb = KB(nc)
    x_in = kb.dram("x_in", [T, D], F32, kind="ExternalInput")
    wqkv = kb.dram("wqkv", [D, 3 * D], F32, kind="ExternalInput")
    wproj = kb.dram("wproj", [D, D], F32, kind="ExternalInput")
    lam_in = kb.dram("lam_in", [1, 256], F32, kind="ExternalInput")
    w_gate = kb.dram("w_gate", [5504, D], F32, kind="ExternalInput")
    w_up = kb.dram("w_up", [5504, D], F32, kind="ExternalInput")
    w_down = kb.dram("w_down", [D, 5504], F32, kind="ExternalInput")
    W1 = {"wq": kb.dram("wq", [D, D], F32, kind="ExternalInput"),
          "wk": kb.dram("wk", [D, D], F32, kind="ExternalInput"),
          "wv": kb.dram("wv", [D, D], F32, kind="ExternalInput"),
          "wo": kb.dram("wo", [D, D], F32, kind="ExternalInput"),
          "wbeta": kb.dram("wbeta", [D, 16], F32, kind="ExternalInput"),
          "ln_w": kb.dram("ln_w", [16, 128], F32, kind="ExternalInput"),
          "ln_b": kb.dram("ln_b", [16, 128], F32, kind="ExternalInput"),
          "wg1": kb.dram("wg1", [D, 128], F32, kind="ExternalInput"),
          "wg2": kb.dram("wg2", [128, D], F32, kind="ExternalInput"),
          "norm_w": kb.dram("norm_w", [1, D], F32, kind="ExternalInput")}
    w_fc = kb.dram("w_fc", [8192, D], F32, kind="ExternalInput")
    w_proj1 = kb.dram("w_proj", [D, 8192], F32, kind="ExternalInput")
    out_d = kb.dram("out_d", [T, D], F32, kind="ExternalOutput")
    x1_s = kb.dram("x1_s", [T, D], F32)
    x2_s = kb.dram("x2_s", [T, D], F32)
    x3_s = kb.dram("x3_s", [T, D], F32)
    kb.cs = load_consts(kb)
    kb.barrier()
    build_l0_attn(kb, x_in, wqkv, wproj, lam_in, x1_s, kb.cs)
    build_l0_ffn(kb, x1_s, w_gate, w_up, w_down, x2_s, kb.cs)
    build_l1_attn(kb, x2_s, W1, x3_s, kb.cs)
    build_l1_ffn(kb, x3_s, w_fc, w_proj1, out_d, kb.cs)
    return kb.finalize()


_NCORES = 8


def _run(nc, im_base, per_core, outname):
    in_maps = []
    for i in range(_NCORES):
        d = dict(im_base)
        d.update({k: v[i] for k, v in per_core.items()})
        in_maps.append(d)
    res = run_bass_kernel_spmd(nc, in_maps, core_ids=list(range(_NCORES)))
    return np.stack([r[outname] for r in res.results], axis=0)


def kernel(**inputs):
    consts = make_consts()
    lam_in = np.concatenate([
        np.asarray(inputs["L0_attn_lambda_q1"]),
        np.asarray(inputs["L0_attn_lambda_k1"]),
        np.asarray(inputs["L0_attn_lambda_q2"]),
        np.asarray(inputs["L0_attn_lambda_k2"])]).reshape(1, 256) \
        .astype(np.float32)
    x = np.ascontiguousarray(np.asarray(inputs["L0_attn_x"], dtype=np.float32))
    nc = build_fused_program()
    im = dict(consts)
    im.update({
        "wqkv": np.asarray(inputs["L0_attn_Wqkv"]),
        "wproj": np.asarray(inputs["L0_attn_Wproj"]),
        "lam_in": lam_in,
        "w_gate": np.asarray(inputs["L0_ffn_w_gate"]),
        "w_up": np.asarray(inputs["L0_ffn_w_up"]),
        "w_down": np.asarray(inputs["L0_ffn_w_down"]),
        "wq": np.asarray(inputs["L1_attn_Wq"]),
        "wk": np.asarray(inputs["L1_attn_Wk"]),
        "wv": np.asarray(inputs["L1_attn_Wv"]),
        "wo": np.asarray(inputs["L1_attn_Wo"]),
        "wbeta": np.asarray(inputs["L1_attn_Wbeta"]),
        "ln_w": np.asarray(inputs["L1_attn_ln_w"]),
        "ln_b": np.asarray(inputs["L1_attn_ln_b"]),
        "wg1": np.asarray(inputs["L1_attn_Wg1"]),
        "wg2": np.asarray(inputs["L1_attn_Wg2"]),
        "norm_w": np.asarray(inputs["L1_attn_norm_w"]).reshape(1, D),
        "w_fc": np.asarray(inputs["L1_ffn_w_fc"]),
        "w_proj": np.asarray(inputs["L1_ffn_w_proj"])})
    return _run(nc, im, {"x_in": x}, "out_d")


# version 05 (final)
# speedup vs baseline: 1.0669x; 1.0669x over previous
import math
from contextlib import ExitStack

import numpy as np
import ml_dtypes

import concourse.bass as bass
import concourse.mybir as mybir
from concourse.bass_utils import run_bass_kernel_spmd

F32 = mybir.dt.float32
BF16 = mybir.dt.bfloat16
I32 = mybir.dt.int32
AF = mybir.ActivationFunctionType
ALU = mybir.AluOpType
AX = mybir.AxisListType

ENGINES = ("tensor", "vector", "scalar", "gpsimd", "sync")

T = 2048
D = 2048
NH = 16
MAGIC = float(np.float32(1.5 * 2 ** 23))
LAMBDA_INIT = 0.8 - 0.6 * math.exp(-0.3)
NEGBIG = -1.0e9
DBG = False


class _Chain:
    __slots__ = ("entry",)

    def __init__(self, entry):
        self.entry = entry

    def then_inc(self, sem, n):
        self.entry[3].append((sem, n))
        return self


def _is_ap(x):
    return type(x).__name__ == "AP"


_OUT_KW = ("out", "accum_out", "out_ap", "dst")
_SEQ_OPS = ("wait_ge", "sem_inc", "nop")


def _op_io(name, a, kw):
    writes, reads = [], []
    if name == "memset" or name == "memzero":
        writes.append(a[0] if a else kw.get("ap"))
        return writes, reads
    if name == "dma_start":
        o = kw.get("out", a[0] if len(a) > 0 else None)
        i = kw.get("in_", a[1] if len(a) > 1 else None)
        if _is_ap(o):
            writes.append(o)
        if _is_ap(i):
            reads.append(i)
        return writes, reads
    first = True
    for x in a:
        if _is_ap(x):
            if first:
                writes.append(x)
                first = False
            else:
                reads.append(x)
    for k, x in kw.items():
        if _is_ap(x):
            if k in _OUT_KW:
                writes.append(x)
            else:
                reads.append(x)
    return writes, reads


class Rec:

    def __init__(self, name):
        self._name = name
        self.calls = []
        self._kb = None
        self._waited = {}

    def __getattr__(self, name):
        if name.startswith("_"):
            raise AttributeError(name)

        def call(*a, **kw):
            entry = [name, a, kw, [], None, self._name]
            if name not in _SEQ_OPS and self._kb is not None:
                self._kb._track_op(self, entry)
            self.calls.append(entry)
            if self._kb is not None and name not in _SEQ_OPS \
                    and name != "dma_start":
                self._kb._pending[self._name] = entry
            return _Chain(entry)

        return call

    def _note_wait(self, sem, val):
        if self._waited.get(id(sem), -1) < val:
            self._waited[id(sem)] = val

    def _emit_wait(self, sem, val):
        if self._waited.get(id(sem), -1) < val:
            self.calls.append(["wait_ge", (sem, val), {}, [], None, self._name])
            self._waited[id(sem)] = val

    def _replay(self, eng):
        for name, a, kw, incs, _tok, _e in self.calls:
            inst = getattr(eng, name)(*a, **kw)
            for sem, n in incs:
                inst = inst.then_inc(sem, n)

    def _last_real(self, since):
        for i in range(len(self.calls) - 1, since - 1, -1):
            if self.calls[i][0] not in ("wait_ge", "sem_inc", "dma_start"):
                return i
        return None


class Edge:

    __slots__ = ("sem", "cnt", "queue")

    def __init__(self, sem):
        self.sem = sem
        self.cnt = 0

    def inc(self, chain, n=1):
        chain.then_inc(self.sem, n)
        self.cnt += n
        if chain.entry[4] is None:
            chain.entry[4] = (self.sem, self.cnt)
        return self.cnt

    def wait(self, rec, cnt=None):
        rec.wait_ge(self.sem, self.cnt if cnt is None else cnt)


class KB:

    def __init__(self, nc):
        self.nc = nc
        self.es = ExitStack()
        self.r = {e: Rec(e) for e in ENGINES}
        self.t = self.r["tensor"]
        self.v = self.r["vector"]
        self.s = self.r["scalar"]
        self.g = self.r["gpsimd"]
        self.y = self.r["sync"]
        self.bar = self.sem("bar")
        self.nphase = 0
        self._phase_start = {e: 0 for e in ENGINES}
        self.dmasem = {e: self.sem("dma_" + e) for e in ("gpsimd", "sync", "scalar")}
        self.dmacnt = {e: 0 for e in ("gpsimd", "sync", "scalar")}
        self.fin = {e: Edge(self.sem("fin_" + e)) for e in ENGINES}
        self._edges = {}
        self._dsems = []
        self._lastw = {}
        self._lastr = {}
        self._pending = {e: None for e in ENGINES}
        self._autogen = {}
        for e in ENGINES:
            self.r[e]._kb = self
        self.dpool = {e: [self.dsem(f"dp_{e}{i}", e) for i in range(6)]
                      for e in ("gpsimd", "sync", "scalar")}
        self.dpi = {e: 0 for e in ("gpsimd", "sync", "scalar")}

    def sem(self, name=None):
        return self.es.enter_context(self.nc.semaphore(name))

    def edge(self, name):
        if name not in self._edges:
            self._edges[name] = Edge(self.sem(name))
        return self._edges[name]

    def _sem_count(self, sem):
        for ed in list(self._edges.values()) + list(self.fin.values()):
            if ed.sem is sem:
                return ed.cnt
        for e, s in self.dmasem.items():
            if s is sem:
                return self.dmacnt[e]
        raise KeyError("untracked semaphore in barrier")

    def sbuf(self, name, shape, dtype=F32, scope=None):
        return (scope or self.es).enter_context(self.nc.sbuf_tensor(name, shape, dtype))

    def psum(self, name, shape, dtype=F32, scope=None):
        return (scope or self.es).enter_context(self.nc.psum_tensor(name, shape, dtype))

    def dram(self, name, shape, dtype=F32, kind="Internal"):
        al = getattr(self, "scratch_alias", None)
        if al and name in al and kind == "Internal":
            ap = al[name]
            assert list(ap.shape) == list(shape), (name, ap.shape, shape)
            return ap
        return self.nc.dram_tensor(name, shape, dtype, kind=kind).ap()

    def dsem(self, name, queue):
        d = Edge(self.sem(name))
        d.queue = queue
        self._dsems.append(d)
        return d

    SEM_CAP = 20000

    def _auto_edge(self, e):
        key = "auto_" + e
        gen = self._autogen.get(e, 0)
        ed = self.edge(key + (str(gen) if gen else ""))
        if ed.cnt >= self.SEM_CAP:
            gen += 1
            self._autogen[e] = gen
            ed = self.edge(key + str(gen))
        return ed

    def _finalize_pending(self, e):
        ent = self._pending.get(e)
        if ent is None:
            return
        self._pending[e] = None
        if ent[4] is not None:
            return
        if ent[3]:
            raise RuntimeError(f"producer {ent[0]}@{e} has untracked inc")
        au = self._auto_edge(e)
        ent[3].append((au.sem, 1))
        au.cnt += 1
        ent[4] = (au.sem, au.cnt)

    def _token(self, ent):
        if ent[4] is None:
            self._finalize_pending(ent[5])
        tok = ent[4]
        assert tok is not None, f"untokened producer {ent[0]}@{ent[5]}"
        return tok

    def _track_op(self, rec, entry):
        self._finalize_pending(rec._name)
        name, a, kw = entry[0], entry[1], entry[2]
        ws, rs = _op_io(name, a, kw)
        ws = [x for x in ws if x is not None and str(x.space.value) != "dram"]
        rs = [x for x in rs if x is not None and str(x.space.value) != "dram"]
        deps = []
        for x in rs + ws:
            deps.extend(self._lastw.get(x.tensor.name, ()))
        for x in ws:
            deps.extend(self._lastr.get(x.tensor.name, ()))
        seen = set()
        for ent in deps:
            if id(ent) in seen or ent is entry:
                continue
            seen.add(id(ent))
            if ent[5] == "tensor" and rec._name == "tensor":
                continue
            sem, val = self._token(ent)
            full = self._dma_full_count(sem)
            if full is not None:
                val = full
            rec._emit_wait(sem, val)
        for w in ws:
            self._lastw.setdefault(w.tensor.name, []).append(entry)
        for x in rs:
            self._lastr.setdefault(x.tensor.name, []).append(entry)

    def _dma_full_count(self, sem):
        for e, s in self.dmasem.items():
            if s is sem:
                return self.dmacnt[e]
        for d in self._dsems:
            if d.sem is sem:
                return d.cnt
        return None

    def dma(self, e, out, in_, ds=None, **kw):
        if kw.get("transpose"):
            e = "sync"
        if ds is None:
            ds = self.dpool[e][self.dpi[e]]
            self.dpi[e] = (self.dpi[e] + 1) % len(self.dpool[e])
        if ds.cnt > 0:
            self.r[e].wait_ge(ds.sem, ds.cnt)
            self.r[e]._note_wait(ds.sem, ds.cnt)
        ch = self.r[e].dma_start(out=out, in_=in_, **kw)
        ch.then_inc(ds.sem, 16)
        ds.cnt += 16
        ch.entry[4] = (ds.sem, ds.cnt)
        return ds.cnt

    def wait_dma(self, consumer, e, cnt=None):
        v = self.dmacnt[e] if cnt is None else cnt
        self.r[consumer].wait_ge(self.dmasem[e], v)
        self.r[consumer]._note_wait(self.dmasem[e], v)

    def barrier(self):
        self.nphase += 1
        tgt = 5 * self.nphase
        for e in ENGINES:
            r = self.r[e]
            for d in self._dsems:
                if d.queue == e and d.cnt > 0:
                    r.wait_ge(d.sem, d.cnt)
            li = r._last_real(self._phase_start[e])
            if li is not None:
                sem, val = self._token(r.calls[li])
                full = self._dma_full_count(sem)
                r.wait_ge(sem, full if full is not None else val)
            r.sem_inc(self.bar, 1)
        for e in ENGINES:
            self.r[e].wait_ge(self.bar, tgt)
            self._phase_start[e] = len(self.r[e].calls)
        self._lastw.clear()
        self._lastr.clear()

    def finalize(self):
        nc = self.nc
        recs = self.r
        with nc.Block() as block:
            @block.tensor
            def _(eng):
                recs["tensor"]._replay(eng)

            @block.vector
            def _(eng):
                recs["vector"]._replay(eng)

            @block.scalar
            def _(eng):
                recs["scalar"]._replay(eng)

            @block.gpsimd
            def _(eng):
                recs["gpsimd"]._replay(eng)

            @block.sync
            def _(eng):
                recs["sync"]._replay(eng)
        self.es.close()
        return nc


class MMStage:

    def __init__(self, kb, banks, name, evac_shape=None, evac_dtype=BF16,
                 dma_q="scalar", scope=None):
        self.kb = kb
        self.banks = banks
        self.n = 0
        self.ebufs = None
        if evac_shape is not None:
            self.ebufs = [kb.sbuf(f"{name}_eb{i}", evac_shape, evac_dtype,
                                  scope=scope)
                          for i in range(2)]
            self.eds = [kb.dsem(f"{name}_ds{i}", dma_q) for i in range(2)]
        self.dma_q = dma_q

    def tile_custom(self, emit_mms, emit_evac):
        bank = self.banks[self.n % len(self.banks)]
        emit_mms(bank)
        emit_evac(bank)
        self.n += 1

    def tile(self, emit_mms, dst_ap):
        kb = self.kb
        i = self.n
        eb = self.ebufs[i % 2]
        bank = self.banks[i % len(self.banks)]
        emit_mms(bank)
        kb.v.tensor_copy(eb[:], bank[:])
        kb.dma(self.dma_q, dst_ap, eb[:], ds=self.eds[i % 2])
        self.n += 1


def emit_ln_rows(kb, x_ap, out_ap, width, eps, tmp, stats):
    v, s = kb.v, kb.s
    tmp = tmp[:, 0:width]
    s1, s2 = stats[:, 0:1], stats[:, 1:2]
    mu, var = stats[:, 2:3], stats[:, 3:4]
    rstd = stats[:, 4:5]
    v.tensor_reduce(s1, x_ap, AX.X, ALU.add)
    s.activation(tmp, x_ap, AF.Square, accum_out=s2)
    v.tensor_scalar(mu, s1, 1.0 / width, None, ALU.mult)
    v.tensor_scalar(s2, s2, 1.0 / width, None, ALU.mult)
    v.tensor_tensor(var, mu, mu, ALU.mult)
    v.tensor_tensor(var, s2, var, ALU.subtract)
    s.activation(rstd, var, AF.Ln, bias=float(eps))
    s.activation(rstd, rstd, AF.Exp, scale=-0.5)
    return v.tensor_scalar(out_ap, x_ap, mu, rstd, ALU.subtract, ALU.mult)


def build_l0_attn(kb, x_in, wqkv, wproj, lam_in, x1_out, cs):
    t, v, s = kb.t, kb.v, kb.s
    dmask = cs["dmask"]
    eh2 = cs["eh2"]
    selrow = cs["selrow"]
    ones_col = cs["ones_col"]
    ones_colb = cs["ones_colb"]
    ones_row = cs["ones_row"]

    dk = "ExternalOutput" if DBG else "Internal"
    scr_xbf = kb.dram("scr_xbf", [T, D], BF16, kind=dk)
    scr_qkT = kb.dram("scr_qkT", [4096, T], BF16, kind=dk)
    scr_vbf = kb.dram("scr_vbf", [T, D], BF16, kind=dk)
    scr_difT = kb.dram("scr_difT", [D, T], BF16, kind=dk)

    kb.dma("gpsimd", scr_xbf, x_in)
    lamv = kb.sbuf("lamv", [1, 256], F32)
    kb.dma("sync", lamv[:], lam_in)
    kb.barrier()

    sc = ExitStack()
    xT = kb.sbuf("xT", [128, 16 * T], BF16, scope=sc)
    for k in range(16):
        kb.dma("sync", xT[:, k * T:(k + 1) * T],
               scr_xbf[:, k * 128:(k + 1) * 128], transpose=True)
    wslab = [kb.sbuf(f"wslab{i}", [128, 16 * 512], BF16, scope=sc) for i in range(2)]
    ws_ds = [kb.dsem(f"ws_ds{i}", "gpsimd") for i in range(2)]
    kb.barrier()

    banks = [kb.psum("psq%d" % i, [128, 512], F32, scope=sc) for i in range(4)]
    st = MMStage(kb, banks, "qkv", evac_shape=[128, 512], scope=sc)
    for slab in range(12):
        wb = wslab[slab % 2]
        n0 = slab * 512
        kb.dma("gpsimd", wb[:].rearrange("p (k n) -> p k n", k=16),
               wqkv[:, n0:n0 + 512].rearrange("(k p) n -> p k n", p=128),
               ds=ws_ds[slab % 2])
        if slab < 8:
            for m in range(4):
                for n in range(4):
                    of_abs = slab * 4 + m

                    def mms(bank, m=m, n=n, wb=wb):
                        for k in range(16):
                            t.matmul(
                                bank[:],
                                wb[:, (k * 512 + m * 128):(k * 512 + m * 128 + 128)],
                                xT[:, (k * T + n * 512):(k * T + n * 512 + 512)],
                                start=(k == 0), stop=(k == 15))

                    st.tile(mms, scr_qkT[of_abs * 128:(of_abs + 1) * 128,
                                         n * 512:(n + 1) * 512])
        else:
            for m in range(16):

                def mms(bank, m=m, wb=wb):
                    for k in range(16):
                        t.matmul(
                            bank[:],
                            xT[:, (k * T + m * 128):(k * T + m * 128 + 128)],
                            wb[:, k * 512:(k + 1) * 512],
                            start=(k == 0), stop=(k == 15))

                st.tile(mms, scr_vbf[m * 128:(m + 1) * 128,
                                     (slab - 8) * 512:(slab - 7) * 512])
    kb.barrier()
    sc.close()

    scd = ExitStack()
    neglam = kb.sbuf("neglam", [128, 1], F32, scope=scd)
    negM = kb.sbuf("negM", [128, 32], F32, scope=scd)
    sc = ExitStack()
    sm = kb.sbuf("sm_l0", [1, 64], F32, scope=sc)
    qk_sb = kb.sbuf("qk_sb", [128, 2048], BF16, scope=sc)
    sqf = kb.sbuf("sqf", [128, 2048], F32, scope=sc)
    mx = kb.sbuf("mx_l0", [2, 48], F32, scope=sc)
    pm = kb.psum("ps_mh", [128, 512], F32, scope=sc)
    pbc = kb.psum("ps_bc", [128, 32], F32, scope=sc)

    v.tensor_tensor(sm[:, 0:64], lamv[:, 0:64], lamv[:, 64:128], ALU.mult)
    v.tensor_reduce(mx[0:1, 40:41], sm[:, 0:64], AX.X, ALU.add)
    v.tensor_tensor(sm[:, 0:64], lamv[:, 128:192], lamv[:, 192:256], ALU.mult)
    v.tensor_reduce(mx[0:1, 41:42], sm[:, 0:64], AX.X, ALU.add)
    s.activation(mx[0:1, 42:44], mx[0:1, 40:42], AF.Exp)
    v.tensor_tensor(mx[0:1, 44:45], mx[0:1, 42:43], mx[0:1, 43:44], ALU.subtract)
    v.tensor_scalar(mx[0:1, 45:46], mx[0:1, 44:45], float(LAMBDA_INIT),
                    -1.0, ALU.add, ALU.mult)
    t.matmul(pbc[:, 0:1], ones_row[:], mx[0:1, 45:46], start=True, stop=True)
    v.tensor_copy(neglam[:], pbc[:, 0:1])

    for tile in range(32):
        kb.dma("sync", qk_sb[:], scr_qkT[tile * 128:(tile + 1) * 128, :])
        s.activation(sqf[:], qk_sb[:], AF.Square)
        for n in range(4):
            t.matmul(pm[0:2, 0:512], eh2[:], sqf[:, n * 512:(n + 1) * 512],
                     start=True, stop=True)
            v.tensor_reduce(mx[0:2, 46:47], pm[0:2, 0:512], AX.X, ALU.max)
            if n == 0:
                v.tensor_copy(mx[0:2, tile:tile + 1], mx[0:2, 46:47])
            else:
                v.tensor_tensor(mx[0:2, tile:tile + 1], mx[0:2, tile:tile + 1],
                                mx[0:2, 46:47], ALU.max)
    for j in range(2):
        q0, k0 = 16 * j, 16 * j + 8
        v.tensor_tensor(mx[0:2, 32:40], mx[0:2, q0:q0 + 8],
                        mx[0:2, k0:k0 + 8], ALU.mult)
        s.activation(mx[0:2, 32:40], mx[0:2, 32:40], AF.Sqrt, scale=1.0 / 64)
        v.tensor_scalar(mx[0:2, 32:40], mx[0:2, 32:40], -1.0, None, ALU.mult)
        for r in range(2):
            t.matmul(pbc[:, 16 * j + 8 * r:16 * j + 8 * r + 8],
                     selrow[:, 128 * r:128 * (r + 1)], mx[0:2, 32:40],
                     start=True, stop=True)
    v.tensor_copy(negM[:], pbc[:])
    if DBG:
        dbgM = kb.dram("dbgM", [128, 32], F32, kind="ExternalOutput")
        kb.dma("gpsimd", dbgM, negM[:])
    kb.barrier()
    sc.close()

    sc = ExitStack()
    kT = kb.sbuf("kT_l0", [128, 16 * T], BF16, scope=sc)
    vsb = kb.sbuf("v_l0", [128, 256 * 128], BF16, scope=sc)
    qsl = kb.sbuf("q_l0", [128, 2 * 512], BF16, scope=sc)
    psb = [kb.sbuf("p_l0%d" % i, [128, 512], BF16, scope=sc) for i in range(2)]
    u_sb = kb.sbuf("u_l0", [128, 512], F32, scope=sc)
    t2_sb = kb.sbuf("t2_l0", [128, 512], F32, scope=sc)
    c1_sb = kb.sbuf("c1_l0", [128, 512], F32, scope=sc)
    c2_sb = kb.sbuf("c2_l0", [128, 512], F32, scope=sc)
    row_sb = kb.sbuf("row_l0", [1, 512], F32, scope=sc)
    dif_bf = [kb.sbuf("dif_l0%d" % i, [128, 512], BF16, scope=sc) for i in range(2)]
    dif_ds = [kb.dsem("dif_ds%d" % i, "scalar") for i in range(2)]
    for i in range(8):
        kb.dma("sync", kT[:, i * T:(i + 1) * T],
               scr_qkT[(8 + i) * 128:(9 + i) * 128, :])
        kb.dma("sync", kT[:, (8 + i) * T:(9 + i) * T],
               scr_qkT[(24 + i) * 128:(25 + i) * 128, :])
    for h in range(16):
        kb.dma("scalar", vsb[:, h * 16 * 128:(h + 1) * 16 * 128]
               .rearrange("p (c f) -> p c f", c=16),
               scr_vbf[:, h * 128:(h + 1) * 128]
               .rearrange("(c p) f -> p c f", p=128))
    kb.barrier()

    psS = [kb.psum("psS%d" % i, [128, 512], F32, scope=sc) for i in range(2)]
    psA1 = kb.psum("psA1", [128, 512], F32, scope=sc)
    psA2 = kb.psum("psA2", [128, 512], F32, scope=sc)
    psl1 = kb.psum("psl1", [1, 512], F32, scope=sc)
    psl2 = kb.psum("psl2", [1, 512], F32, scope=sc)
    psE = kb.psum("psE", [128, 512], F32, scope=sc)
    qs_ds = kb.dsem("qs_ds", "sync")

    for h in range(16):
        ktile = h // 2
        kbase = 64 * (h % 2)
        for tq in range(4):
            kb.dma("sync", qsl[kbase:kbase + 64, 0:512],
                   scr_qkT[h * 64:h * 64 + 64, tq * 512:(tq + 1) * 512],
                   ds=qs_ds)
            kb.dma("sync", qsl[kbase:kbase + 64, 512:1024],
                   scr_qkT[2048 + h * 64:2048 + h * 64 + 64,
                           tq * 512:(tq + 1) * 512], ds=qs_ds)
            nch = 4 * tq + 4
            for j in range(2):
                psA = psA1 if j == 0 else psA2
                psl = psl1 if j == 0 else psl2
                for c in range(nch):
                    pi = (j * nch + c) % 2
                    lhsT = kT[kbase:kbase + 64,
                              (8 * j + ktile) * T + c * 128:
                              (8 * j + ktile) * T + c * 128 + 128]
                    rhs = qsl[kbase:kbase + 64, j * 512:j * 512 + 512]
                    t.matmul(psS[pi][:], lhsT, rhs, start=True, stop=True,
                             skip_group_check=True)
                    dloc = c - (nch - 4)
                    if dloc >= 0:
                        v.tensor_tensor(psS[pi][:], psS[pi][:],
                                        dmask[:, dloc * 512:(dloc + 1) * 512],
                                        ALU.add)
                    bcol = j * 16 + (h % 2) * 8 + h // 2
                    s.activation(psb[pi][:], psS[pi][:], AF.Exp,
                                 bias=negM[:, bcol:bcol + 1], scale=0.125)
                    if DBG and (h, tq, j, c) == (0, 0, 0, 0):
                        dbgP = kb.dram("dbgP", [128, 512], BF16,
                                       kind="ExternalOutput")
                        kb.dma("gpsimd", dbgP, psb[pi][:])
                    t.matmul(psA[:], vsb[:, (h * 16 + c) * 128:
                                         (h * 16 + c + 1) * 128],
                             psb[pi][:], start=(c == 0), stop=(c == nch - 1),
                             skip_group_check=True)
                    t.matmul(psl[:], ones_colb[:], psb[pi][:],
                             start=(c == 0), stop=(c == nch - 1),
                             skip_group_check=True)
            for j in range(2):
                v.reciprocal(row_sb[:], (psl1 if j == 0 else psl2)[:])
                t.matmul(psE[:], ones_row[:], row_sb[:], start=True, stop=True)
                s.activation(c1_sb[:] if j == 0 else c2_sb[:], psE[:], AF.Copy)
            v.tensor_tensor(u_sb[:], psA1[:], c1_sb[:], ALU.mult)
            v.tensor_tensor(t2_sb[:], psA2[:], c2_sb[:], ALU.mult)
            v.scalar_tensor_tensor(u_sb[:], t2_sb[:], neglam[:, 0:1],
                                   u_sb[:], ALU.mult, ALU.add)
            s.activation(t2_sb[:], u_sb[:], AF.Square)
            t.matmul(psE[0:1, :], ones_col[:], t2_sb[:], start=True, stop=True)
            s.activation(row_sb[:], psE[0:1, :], AF.Ln, bias=1e-5,
                         scale=1.0 / 128)
            s.activation(row_sb[:], row_sb[:], AF.Exp, scale=-0.5)
            t.matmul(psE[:], ones_row[:], row_sb[:], start=True, stop=True)
            s.activation(c1_sb[:], psE[:], AF.Copy,
                         scale=float(1.0 - LAMBDA_INIT))
            bi = (h * 4 + tq) % 2
            db = dif_bf[bi]
            v.tensor_tensor(db[:], u_sb[:], c1_sb[:], ALU.mult)
            kb.dma("scalar", scr_difT[h * 128:(h + 1) * 128,
                                      tq * 512:(tq + 1) * 512], db[:],
                   ds=dif_ds[bi])
    kb.barrier()
    sc.close()
    scd.close()

    sc = ExitStack()
    difT = kb.sbuf("difT_sb", [128, 16 * T], BF16, scope=sc)
    wpj = kb.sbuf("wpj", [128, 16 * T], BF16, scope=sc)
    xres = kb.sbuf("xres_l0", [128, 2048], F32, scope=sc)
    rrow = kb.sbuf("rrow_l0", [128, 2048], F32, scope=sc)
    ltmp = kb.sbuf("ltmp_l0", [128, 2048], F32, scope=sc)
    lstat = kb.sbuf("lstat_l0", [128, 8], F32, scope=sc)
    xres_ds = kb.dsem("xres_ds", "sync")
    x1_ds = kb.dsem("x1_ds", "gpsimd")
    for k in range(16):
        kb.dma("sync", difT[:, k * T:(k + 1) * T],
               scr_difT[k * 128:(k + 1) * 128, :])
    kb.dma("gpsimd", wpj[:].rearrange("p (k n) -> p k n", k=16),
           wproj.rearrange("(k p) n -> p k n", p=128))
    kb.barrier()
    banks = [kb.psum("pspj%d" % i, [128, 512], F32, scope=sc) for i in range(4)]
    stp = MMStage(kb, banks, "proj")
    for m in range(16):
        kb.dma("sync", xres[:], x_in[m * 128:(m + 1) * 128, :], ds=xres_ds)
        for n in range(4):

            def mms(bank, m=m, n=n):
                for k in range(16):
                    t.matmul(bank[:],
                             difT[:, k * T + m * 128:k * T + m * 128 + 128],
                             wpj[:, k * T + n * 512:k * T + n * 512 + 512],
                             start=(k == 0), stop=(k == 15))

            def evac(bank, n=n):
                v.tensor_tensor(rrow[:, n * 512:(n + 1) * 512], bank[:],
                                xres[:, n * 512:(n + 1) * 512], ALU.add)

            stp.tile_custom(mms, evac)
        emit_ln_rows(kb, rrow[:], ltmp[:], 2048, 1e-5, ltmp, lstat)
        kb.dma("gpsimd", x1_out[m * 128:(m + 1) * 128, :], ltmp[:, 0:2048],
               ds=x1_ds)
    kb.barrier()
    sc.close()


def emit_actquant_rows(kb, x_ap, out_bf_ap, sinv_col, qs, W, eps=1e-5):
    v, g = kb.v, kb.g
    amax, r = qs["st"][:, 0:1], qs["st"][:, 1:2]
    s127 = qs["st"][:, 2:3]
    t_, nf = qs["t"][:, 0:W], qs["nf"][:, 0:W]
    b1, b2 = qs["b1"][:, 0:W], qs["b2"][:, 0:W]
    i32 = qs["i32"][:, 0:W]
    v.tensor_reduce(amax, x_ap, AX.X, ALU.max, apply_absolute_value=True)
    v.tensor_scalar(amax, amax, float(eps), None, ALU.max)
    v.reciprocal(r, amax)
    v.tensor_scalar(s127, r, 127.0, None, ALU.mult)
    v.tensor_scalar(sinv_col, amax, 1.0 / 127.0, None, ALU.mult)
    v.tensor_scalar(t_, x_ap, s127, None, ALU.mult)
    g.tensor_scalar(nf, t_, 0.5 + MAGIC, MAGIC, ALU.add, ALU.subtract)
    g.tensor_copy(i32, nf)
    g.tensor_copy(nf, i32)
    v.scalar_tensor_tensor(b1, t_, 1.5, nf, ALU.add, ALU.is_ge)
    v.scalar_tensor_tensor(b2, t_, 0.5, nf, ALU.add, ALU.is_ge)
    v.tensor_tensor(b1, b1, b2, ALU.add)
    v.scalar_tensor_tensor(b2, t_, -0.5, nf, ALU.add, ALU.is_ge)
    v.scalar_tensor_tensor(b1, b2, 2.0, b1, ALU.subtract, ALU.add)
    return v.tensor_tensor(out_bf_ap, b1, nf, ALU.add)


def emit_tern_group(kb, w_dram, out_dram, O, I, sc, tag, eps=1e-5):
    v = kb.v
    ng = I // 128
    wt = kb.sbuf(f"tw_{tag}", [128, I], F32, scope=sc)
    a = kb.sbuf(f"ta_{tag}", [128, I], F32, scope=sc)
    b = kb.sbuf(f"tb_{tag}", [128, I], F32, scope=sc)
    wq = kb.sbuf(f"tq_{tag}", [128, I], BF16, scope=sc)
    ss = kb.sbuf(f"ts_{tag}", [128, 3 * ng], F32, scope=sc)
    wq_ds = kb.dsem(f"twd_{tag}", "scalar")
    for rt in range(O // 128):
        kb.dma("sync", wt[:], w_dram[rt * 128:(rt + 1) * 128, :])
        w3 = wt[:].rearrange("p (a b) -> p a b", a=ng)
        sums = ss[:, 0:ng]
        thr = ss[:, ng:2 * ng]
        nthr = ss[:, 2 * ng:3 * ng]
        v.tensor_reduce(sums, w3, AX.X, ALU.add, apply_absolute_value=True)
        v.tensor_scalar(thr, sums, 1.0 / 256.0, float(eps / 2), ALU.mult, ALU.add)
        v.tensor_scalar(nthr, thr, -1.0, None, ALU.mult)
        a3 = a[:].rearrange("p (a b) -> p a b", a=ng)
        b3 = b[:].rearrange("p (a b) -> p a b", a=ng)
        v.tensor_tensor(a3, w3, thr.unsqueeze(-1).broadcast_to([128, ng, 128]),
                        ALU.is_ge)
        v.tensor_tensor(b3, w3, nthr.unsqueeze(-1).broadcast_to([128, ng, 128]),
                        ALU.is_ge)
        v.tensor_tensor(a[:], a[:], b[:], ALU.add)
        v.tensor_scalar(thr, thr, 2.0, None, ALU.mult)
        v.scalar_tensor_tensor(
            wq[:].rearrange("p (a b) -> p a b", a=ng), a3, 1.0,
            thr.unsqueeze(-1).broadcast_to([128, ng, 128]),
            ALU.subtract, ALU.mult)
        kb.dma("scalar", out_dram[rt * 128:(rt + 1) * 128, :], wq[:], ds=wq_ds)


def build_l0_ffn(kb, x1_in, w_gate, w_up, w_down, x2_out, cs):
    t, v, s, g = kb.t, kb.v, kb.s, kb.g
    DFF = 5504
    NKD = DFF // 128
    dk = "ExternalOutput" if DBG else "Internal"
    scr_xq = kb.dram("f0_xq", [T, D], BF16, kind=dk)
    scr_wgq = kb.dram("f0_wgq", [DFF, D], BF16, kind=dk)
    scr_wuq = kb.dram("f0_wuq", [DFF, D], BF16, kind=dk)
    scr_wdq = kb.dram("f0_wdq", [D, DFF], BF16, kind=dk)
    scr_hid = kb.dram("f0_hid", [T, DFF], F32, kind=dk)
    scr_hq = kb.dram("f0_hq", [T, DFF], BF16, kind=dk)
    scr_pd = kb.dram("f0_pd", [T, D], F32, kind=dk)
    sinv_x = kb.sbuf("f0_sinvx", [128, 16], F32)
    sinv_h = kb.sbuf("f0_sinvh", [128, 16], F32)

    sc = ExitStack()
    xrow = kb.sbuf("f0_xrow", [128, D], F32, scope=sc)
    qs = {"t": kb.sbuf("f0_qt", [128, D], F32, scope=sc),
          "nf": kb.sbuf("f0_qn", [128, D], F32, scope=sc),
          "b1": kb.sbuf("f0_qb1", [128, D], F32, scope=sc),
          "b2": kb.sbuf("f0_qb2", [128, D], F32, scope=sc),
          "i32": kb.sbuf("f0_qi", [128, D], I32, scope=sc),
          "st": kb.sbuf("f0_qs", [128, 4], F32, scope=sc)}
    xq_bf = kb.sbuf("f0_xqb", [128, D], BF16, scope=sc)
    xq_ds = kb.dsem("f0_xqds", "scalar")
    for m in range(16):
        kb.dma("sync", xrow[:], x1_in[m * 128:(m + 1) * 128, :])
        emit_actquant_rows(kb, xrow[:], xq_bf[:], sinv_x[:, m:m + 1], qs, D)
        kb.dma("scalar", scr_xq[m * 128:(m + 1) * 128, :], xq_bf[:], ds=xq_ds)
    kb.barrier()
    sc.close()

    sc = ExitStack()
    emit_tern_group(kb, w_gate, scr_wgq, DFF, D, sc, "wg")
    emit_tern_group(kb, w_up, scr_wuq, DFF, D, sc, "wu")
    emit_tern_group(kb, w_down, scr_wdq, D, DFF, sc, "wd")
    kb.barrier()
    sc.close()

    sc = ExitStack()
    xqT = kb.sbuf("f0_xqT", [128, 16 * T], BF16, scope=sc)
    for k in range(16):
        kb.dma("sync", xqT[:, k * T:(k + 1) * T],
               scr_xq[:, k * 128:(k + 1) * 128], transpose=True)
    wgT = [kb.sbuf("f0_wgT%d" % i, [128, 16 * 512], BF16, scope=sc) for i in range(2)]
    wuT = [kb.sbuf("f0_wuT%d" % i, [128, 16 * 512], BF16, scope=sc) for i in range(2)]
    wg_ds = [kb.dsem("f0_wgds%d" % i, "sync") for i in range(2)]
    wu_ds = [kb.dsem("f0_wuds%d" % i, "scalar") for i in range(2)]
    sg_sb = kb.sbuf("f0_sg", [128, 512], F32, scope=sc)
    hid_sb = [kb.sbuf("f0_hid%d" % i, [128, 512], F32, scope=sc) for i in range(2)]
    hid_ds = [kb.dsem("f0_hidds%d" % i, "scalar") for i in range(2)]
    kb.barrier()
    psA = kb.psum("f0_psA", [128, 512], F32, scope=sc)
    psB = kb.psum("f0_psB", [128, 512], F32, scope=sc)
    slabs = [(i * 512, 512) for i in range(10)] + [(5120, 384)]
    for si, (s0, n) in enumerate(slabs):
        wgb, wub = wgT[si % 2], wuT[si % 2]
        for k in range(16):
            kb.dma("sync", wgb[:, k * 512:k * 512 + n],
                   scr_wgq[s0:s0 + n, k * 128:(k + 1) * 128], transpose=True,
                   ds=wg_ds[si % 2])
            kb.dma("scalar", wub[:, k * 512:k * 512 + n],
                   scr_wuq[s0:s0 + n, k * 128:(k + 1) * 128], transpose=True,
                   ds=wu_ds[si % 2])
        for m in range(16):
            for k in range(16):
                t.matmul(psA[:, 0:n], xqT[:, k * T + m * 128:k * T + m * 128 + 128],
                         wgb[:, k * 512:k * 512 + n],
                         start=(k == 0), stop=(k == 15), skip_group_check=True)
            for k in range(16):
                t.matmul(psB[:, 0:n], xqT[:, k * T + m * 128:k * T + m * 128 + 128],
                         wub[:, k * 512:k * 512 + n],
                         start=(k == 0), stop=(k == 15), skip_group_check=True)
            s.activation(sg_sb[:, 0:n], psA[:, 0:n], AF.Silu,
                         scale=sinv_x[:, m:m + 1])
            hb = hid_sb[m % 2]
            v.scalar_tensor_tensor(hb[:, 0:n], psB[:, 0:n], sinv_x[:, m:m + 1],
                                   sg_sb[:, 0:n], ALU.mult, ALU.mult)
            kb.dma("scalar", scr_hid[m * 128:(m + 1) * 128, s0:s0 + n],
                   hb[:, 0:n], ds=hid_ds[m % 2])
    kb.barrier()
    sc.close()

    sc = ExitStack()
    hrow = kb.sbuf("f0_hrow", [128, DFF], F32, scope=sc)
    qs = {"t": kb.sbuf("f0_ht", [128, DFF], F32, scope=sc),
          "nf": kb.sbuf("f0_hn", [128, DFF], F32, scope=sc),
          "b1": kb.sbuf("f0_hb1", [128, DFF], F32, scope=sc),
          "b2": kb.sbuf("f0_hb2", [128, DFF], F32, scope=sc),
          "i32": kb.sbuf("f0_hi", [128, DFF], I32, scope=sc),
          "st": kb.sbuf("f0_hs", [128, 4], F32, scope=sc)}
    hq_bf = kb.sbuf("f0_hqb", [128, DFF], BF16, scope=sc)
    hq_ds = kb.dsem("f0_hqds", "scalar")
    for m in range(16):
        kb.dma("sync", hrow[:], scr_hid[m * 128:(m + 1) * 128, :])
        emit_actquant_rows(kb, hrow[:], hq_bf[:], sinv_h[:, m:m + 1], qs, DFF)
        kb.dma("scalar", scr_hq[m * 128:(m + 1) * 128, :], hq_bf[:], ds=hq_ds)
    kb.barrier()
    sc.close()

    KH = [(0, 15), (15, 14), (29, 14)]
    for half, (k0, nk) in enumerate(KH):
        sc = ExitStack()
        hqT = kb.sbuf("f0_hqT%d" % half, [128, nk * T], BF16, scope=sc)
        wdT = kb.sbuf("f0_wdT%d" % half, [128, nk * T], BF16, scope=sc)
        for k in range(nk):
            kb.dma("sync", hqT[:, k * T:(k + 1) * T],
                   scr_hq[:, (k0 + k) * 128:(k0 + k + 1) * 128], transpose=True)
            kb.dma("scalar", wdT[:, k * T:(k + 1) * T],
                   scr_wdq[:, (k0 + k) * 128:(k0 + k + 1) * 128],
                   transpose=True)
        kb.barrier()
        if DBG and half == 0:
            dbg_hqT = kb.dram("dbg_hqT", [128, T], BF16, kind="ExternalOutput")
            dbg_wdT = kb.dram("dbg_wdT", [128, T], BF16, kind="ExternalOutput")
            kb.dma("gpsimd", dbg_hqT, hqT[:, 0:T])
            kb.dma("gpsimd", dbg_wdT, wdT[:, 0:T])
        banks = [kb.psum("f0_pd%d_%d" % (half, i), [128, 512], F32, scope=sc)
                 for i in range(4)]
        prow = kb.sbuf("f0_prow%d" % half, [128, D], F32, scope=sc)
        orow = kb.sbuf("f0_orow%d" % half, [128, D], F32, scope=sc)
        x1r = kb.sbuf("f0_x1r%d" % half, [128, D], F32, scope=sc)
        lstat = kb.sbuf("f0_lst%d" % half, [128, 8], F32, scope=sc)
        pr_ds = kb.dsem("f0_prds%d" % half, "scalar")
        x2_ds = kb.dsem("f0_x2ds%d" % half, "gpsimd")
        st2 = MMStage(kb, banks, "f0dn%d" % half)
        for m in range(16):
            if half > 0:
                kb.dma("sync", prow[:], scr_pd[m * 128:(m + 1) * 128, :])
            if half == len(KH) - 1:
                kb.dma("sync", x1r[:], x1_in[m * 128:(m + 1) * 128, :])
            for n in range(4):

                def mms(bank, m=m, n=n):
                    for k in range(nk):
                        t.matmul(bank[:],
                                 hqT[:, k * T + m * 128:k * T + m * 128 + 128],
                                 wdT[:, k * T + n * 512:k * T + n * 512 + 512],
                                 start=(k == 0), stop=(k == nk - 1),
                                 skip_group_check=True)

                if half == 0:
                    def evac(bank, m=m, n=n):
                        v.tensor_copy(orow[:, n * 512:(n + 1) * 512], bank[:])
                else:
                    def evac(bank, m=m, n=n):
                        v.tensor_tensor(orow[:, n * 512:(n + 1) * 512],
                                        bank[:], prow[:, n * 512:(n + 1) * 512],
                                        ALU.add)
                st2.tile_custom(mms, evac)
            if half < len(KH) - 1:
                kb.dma("scalar", scr_pd[m * 128:(m + 1) * 128, :], orow[:],
                       ds=pr_ds)
            else:
                v.scalar_tensor_tensor(orow[:], orow[:], sinv_h[:, m:m + 1],
                                       x1r[:], ALU.mult, ALU.add)
                emit_ln_rows(kb, orow[:], orow[:], D, 1e-5, prow, lstat)
                kb.dma("gpsimd", x2_out[m * 128:(m + 1) * 128, :], orow[:],
                       ds=x2_ds)
        kb.barrier()
        sc.close()


def build_l1_attn(kb, x2_in, W, x3_out, cs):
    t, v, s, g = kb.t, kb.v, kb.s, kb.g
    SEG = 512
    NSEG = T // SEG
    ident = cs["ident"]
    ones_row = cs["ones_row"]
    cmq = cs["cmq"]
    cmk = cs["cmk"]
    dk = "ExternalOutput" if DBG else "Internal"
    scr_x2bf = kb.dram("a1_x2bf", [T, D], BF16, kind=dk)
    scr_q = kb.dram("a1_q", [T, D], F32, kind=dk)
    scr_qbf = kb.dram("a1_qbf", [T, D], BF16, kind=dk)
    scr_knf = kb.dram("a1_knf", [T, D], F32, kind=dk)
    scr_kn = kb.dram("a1_kn", [T, D], BF16, kind=dk)
    scr_ktil = kb.dram("a1_ktil", [T, D], BF16, kind=dk)
    scr_bmt = kb.dram("a1_bmt", [T, D], F32, kind=dk)
    scr_sg = kb.dram("a1_sg", [T, D], F32, kind=dk)
    scr_og = kb.dram("a1_og", [T, D], BF16, kind=dk)

    wT = kb.sbuf("a1_wT", [128, 16], F32)
    bT = kb.sbuf("a1_bT", [128, 16], F32)
    nwc = kb.sbuf("a1_nwc", [128, 16], F32)
    e_tm = kb.sbuf("a1_etm", [128, 16 * 16], F32)
    inv128c = kb.sbuf("a1_i128", [128, 1], F32)

    def etv(m):
        return e_tm[:, m * 16:(m + 1) * 16].unsqueeze(-1) \
            .broadcast_to([128, 16, 128])

    kb.dma("gpsimd", scr_x2bf, x2_in)
    lnw_r = kb.sbuf("a1_lnwr", [16, 128], F32)
    lnb_r = kb.sbuf("a1_lnbr", [16, 128], F32)
    nw_r = kb.sbuf("a1_nwr", [16, 128], F32)
    kb.dma("sync", lnw_r[:], W["ln_w"])
    kb.dma("sync", lnb_r[:], W["ln_b"])
    kb.dma("sync", nw_r[:], W["norm_w"].rearrange("o (a b) -> (o a) b", b=128))
    v.memset(inv128c[:], 1.0 / 128.0)
    kb.barrier()

    sc = ExitStack()
    ps0 = kb.psum("a1_ps0", [128, 512], F32, scope=sc)
    psT = kb.psum("a1_psT", [128, 128], F32, scope=sc)
    t.transpose(psT[:, 0:16], lnw_r[:], ident[0:16, 0:16])
    v.tensor_copy(wT[:], psT[:, 0:16])
    t.transpose(psT[:, 16:32], lnb_r[:], ident[0:16, 0:16])
    v.tensor_copy(bT[:], psT[:, 16:32])
    t.transpose(psT[:, 32:48], nw_r[:], ident[0:16, 0:16])
    v.tensor_copy(nwc[:], psT[:, 32:48])

    xT = kb.sbuf("a1_xT", [128, 16 * T], BF16, scope=sc)
    for k in range(16):
        kb.dma("sync", xT[:, k * T:(k + 1) * T],
               scr_x2bf[:, k * 128:(k + 1) * 128], transpose=True)
    wsm = kb.sbuf("a1_wsm", [128, 16 * 128], BF16, scope=sc)
    kb.dma("gpsimd", wsm[:, 0:16 * 16].rearrange("p (k n) -> p k n", k=16),
           W["wbeta"].rearrange("(k p) n -> p k n", p=128))
    kb.barrier()
    for m in range(16):
        for k in range(16):
            t.matmul(ps0[:, 0:16], xT[:, k * T + m * 128:k * T + m * 128 + 128],
                     wsm[:, k * 16:(k + 1) * 16], start=(k == 0), stop=(k == 15))
        s.activation(e_tm[:, m * 16:(m + 1) * 16], ps0[:, 0:16], AF.Sigmoid)
        v.tensor_scalar(e_tm[:, m * 16:(m + 1) * 16],
                        e_tm[:, m * 16:(m + 1) * 16], 2.0, None, ALU.mult)
    kb.dma("gpsimd", wsm[:].rearrange("p (k n) -> p k n", k=16),
           W["wg1"].rearrange("(k p) n -> p k n", p=128))
    g1T = kb.sbuf("a1_g1T", [128, T], BF16, scope=sc)
    g1f = kb.sbuf("a1_g1f", [128, 128], F32, scope=sc)
    for m in range(16):
        for k in range(16):
            t.matmul(ps0[:, 0:128], xT[:, k * T + m * 128:k * T + m * 128 + 128],
                     wsm[:, k * 128:(k + 1) * 128], start=(k == 0), stop=(k == 15))
        v.tensor_copy(g1f[:], ps0[:, 0:128])
        t.transpose(psT[:], g1f[:], ident[:])
        v.tensor_copy(g1T[:, m * 128:(m + 1) * 128], psT[:])
    kb.dma("gpsimd", wsm[:, 0:4 * 512].rearrange("p (k n) -> p k n", k=4),
           W["wg2"].rearrange("p (k n) -> p k n", k=4))
    sgrow = kb.sbuf("a1_sgrow", [128, D], F32, scope=sc)
    kb.barrier()
    for m in range(16):
        for n in range(4):
            t.matmul(ps0[:, 0:512], g1T[:, m * 128:(m + 1) * 128],
                     wsm[:, n * 512:(n + 1) * 512], start=True, stop=True)
            s.activation(sgrow[:, n * 512:(n + 1) * 512], ps0[:, 0:512],
                         AF.Sigmoid)
        kb.dma("scalar", scr_sg[m * 128:(m + 1) * 128, :], sgrow[:])
    kb.barrier()
    sc.close()

    sc = ExitStack()
    xT2 = kb.sbuf("a1_xT2", [128, 16 * T], BF16, scope=sc)
    for k in range(16):
        kb.dma("sync", xT2[:, k * T:(k + 1) * T],
               scr_x2bf[:, k * 128:(k + 1) * 128], transpose=True)
    wbig = kb.sbuf("a1_wbig", [128, 16 * T], BF16, scope=sc)
    krow = kb.sbuf("a1_krow", [128, D], F32, scope=sc)
    ksq = kb.sbuf("a1_ksq", [128, D], F32, scope=sc)
    kst = kb.sbuf("a1_kst", [128, 48], F32, scope=sc)
    kbf = kb.sbuf("a1_kbf", [128, D], BF16, scope=sc)
    brow = kb.sbuf("a1_brow", [128, D], F32, scope=sc)
    banks = [kb.psum("a1_pk%d" % i, [128, 512], F32, scope=sc) for i in range(4)]
    kb.dma("gpsimd", wbig[:].rearrange("p (k n) -> p k n", k=16),
           W["wk"].rearrange("(k p) n -> p k n", p=128))
    kb.barrier()

    def proj_mms(bank, m, n):
        for k in range(16):
            t.matmul(bank[:], xT2[:, k * T + m * 128:k * T + m * 128 + 128],
                     wbig[:, k * T + n * 512:k * T + n * 512 + 512],
                     start=(k == 0), stop=(k == 15))

    stk = MMStage(kb, banks, "a1k")
    for m in range(16):
        for n in range(4):
            stk.tile_custom(lambda bank, m=m, n=n: proj_mms(bank, m, n),
                            lambda bank, m=m, n=n:
                            v.tensor_copy(krow[:, n * 512:(n + 1) * 512], bank[:]))
        k3 = krow[:].rearrange("p (h e) -> p h e", h=16)
        s.activation(ksq[:], krow[:], AF.Square)
        v.tensor_reduce(kst[:, 0:16], ksq[:].rearrange("p (h e) -> p h e", h=16),
                        AX.X, ALU.add)
        s.activation(kst[:, 16:32], kst[:, 0:16], AF.Ln, bias=1e-6)
        s.activation(kst[:, 16:32], kst[:, 16:32], AF.Exp, scale=-0.5)
        v.tensor_tensor(k3, k3, kst[:, 16:32].unsqueeze(-1)
                        .broadcast_to([128, 16, 128]), ALU.mult)
        kb.dma("scalar", scr_knf[m * 128:(m + 1) * 128, :], krow[:])
        v.tensor_copy(kbf[:], krow[:])
        kb.dma("scalar", scr_kn[m * 128:(m + 1) * 128, :], kbf[:])
        v.tensor_tensor(kbf[:].rearrange("p (h e) -> p h e", h=16), k3,
                        etv(m), ALU.mult)
        kb.dma("scalar", scr_ktil[m * 128:(m + 1) * 128, :], kbf[:])
    kb.barrier()

    kb.dma("gpsimd", wbig[:].rearrange("p (k n) -> p k n", k=16),
           W["wv"].rearrange("(k p) n -> p k n", p=128))
    bflat = kb.sbuf("a1_bflat", [1, D], F32, scope=sc)
    kb.dma("sync", bflat[:], W["ln_b"].rearrange("a b -> (a b)").unsqueeze(0))
    b_bc = kb.sbuf("a1_bbc", [128, D], F32, scope=sc)
    kb.barrier()
    for n in range(4):
        t.matmul(banks[n][:], ones_row[:], bflat[:, n * 512:(n + 1) * 512],
                 start=True, stop=True)
        v.tensor_copy(b_bc[:, n * 512:(n + 1) * 512], banks[n][:])
    stv = MMStage(kb, banks, "a1v")
    for m in range(16):
        kb.dma("sync", krow[:], scr_knf[m * 128:(m + 1) * 128, :])
        for n in range(4):
            def evv(bank, m=m, n=n):
                sl = slice(n * 512, (n + 1) * 512)
                v.tensor_tensor(brow[:, sl], b_bc[:, sl], bank[:], ALU.subtract)
                v.tensor_tensor(brow[:, sl], brow[:, sl], krow[:, sl], ALU.add)
            stv.tile_custom(lambda bank, m=m, n=n: proj_mms(bank, m, n), evv)
        kb.dma("scalar", scr_bmt[m * 128:(m + 1) * 128, :], brow[:])
    kb.barrier()

    kb.dma("gpsimd", wbig[:].rearrange("p (k n) -> p k n", k=16),
           W["wq"].rearrange("(k p) n -> p k n", p=128))
    kb.barrier()
    stq = MMStage(kb, banks, "a1q")
    for m in range(16):
        for n in range(4):
            stq.tile_custom(lambda bank, m=m, n=n: proj_mms(bank, m, n),
                            lambda bank, m=m, n=n:
                            s.activation(krow[:, n * 512:(n + 1) * 512], bank[:],
                                         AF.Silu))
        kb.dma("scalar", scr_q[m * 128:(m + 1) * 128, :], krow[:])
    kb.barrier()
    sc.close()
    kb.dma("gpsimd", scr_qbf, scr_q)
    kb.barrier()

    sc = ExitStack()
    S_sb = kb.sbuf("a1_S", [128, 16 * 128], F32, scope=sc)
    S_bf = kb.sbuf("a1_Sbf", [128, 16 * 128], BF16, scope=sc)
    v.memset(S_sb[:], 0.0)
    v.memset(S_bf[:], 0.0)
    stag = kb.sbuf("a1_stag", [128, 512], F32, scope=sc)
    v.memset(stag[:], 0.0)
    ktp = kb.sbuf("a1_ktp", [128, D], BF16, scope=sc)
    v.memset(ktp[:], 0.0)
    ktilpad = kb.sbuf("a1_kpd", [128, 16 * 128], BF16, scope=sc)
    v.memset(ktilpad[:], 0.0)
    vstk = kb.sbuf("a1_vstk", [128, 16 * 128], BF16, scope=sc)
    v.memset(vstk[:], 0.0)
    khb_sb = kb.sbuf("a1_khb", [128, 16 * 64], F32, scope=sc)
    qhb_sb = kb.sbuf("a1_qhb", [128, 16 * 64], F32, scope=sc)
    cqT = kb.sbuf("a1_cqT", [128, 16 * 64], BF16, scope=sc)
    ckT = kb.sbuf("a1_ckT", [128, 16 * 64], BF16, scope=sc)
    kh_w = kb.sbuf("a1_khw", [128, 256], F32, scope=sc)
    khh = kb.sbuf("a1_khh", [128, 256], F32, scope=sc)
    gg_t = kb.sbuf("a1_gg", [128, 256], F32, scope=sc)
    gw_t = kb.sbuf("a1_gw", [128, 256], F32, scope=sc)
    pp_t = kb.sbuf("a1_pp", [128, 256], F32, scope=sc)
    t1_t = kb.sbuf("a1_t1", [128, 256], F32, scope=sc)
    t2_t = kb.sbuf("a1_t2", [128, 256], F32, scope=sc)
    sq_t = kb.sbuf("a1_sq", [128, 256], F32, scope=sc)
    rows = kb.sbuf("a1_rows", [1, 512], F32, scope=sc)
    rows2 = kb.sbuf("a1_rows2", [1, 512], F32, scope=sc)
    bmt_em = kb.sbuf("a1_bmem", [128, 16 * SEG], F32, scope=sc)
    doT = kb.sbuf("a1_doT", [128, 16 * SEG], F32, scope=sc)
    knT_s = kb.sbuf("a1_knT", [128, 16 * SEG], BF16, scope=sc)
    ktT_s = kb.sbuf("a1_ktT", [128, 16 * SEG], BF16, scope=sc)
    qT_s = kb.sbuf("a1_qT", [128, 16 * SEG], BF16, scope=sc)
    brow2 = kb.sbuf("a1_brow2", [128, D], F32, scope=sc)
    wv3 = wT[:].unsqueeze(-1).broadcast_to([128, 16, 16])
    bv3 = bT[:].unsqueeze(-1).broadcast_to([128, 16, 16])

    psK = kb.psum("a1_psK", [128, 512], F32, scope=sc)
    psK2 = kb.psum("a1_psK2", [128, 512], F32, scope=sc)
    psQ = kb.psum("a1_psQ", [128, 512], F32, scope=sc)
    psQ2 = kb.psum("a1_psQ2", [128, 512], F32, scope=sc)
    psC = kb.psum("a1_psC", [128, 512], F32, scope=sc)
    psR = kb.psum("a1_psR", [1, 512], F32, scope=sc)
    psB = kb.psum("a1_psB", [128, 512], F32, scope=sc)
    psX = kb.psum("a1_psX", [128, 512], F32, scope=sc)

    def seg_chunk_view(buf, h_stride, tok_off, width):
        return buf[:].rearrange("p (h s) -> p h s", h=16)[:, :,
                                                          tok_off:tok_off + width]

    for seg in range(NSEG):
        t0s = seg * SEG
        for k in range(16):
            kb.dma("sync", knT_s[:, k * SEG:(k + 1) * SEG],
                   scr_kn[t0s:t0s + SEG, k * 128:(k + 1) * 128], transpose=True)
            kb.dma("sync", ktT_s[:, k * SEG:(k + 1) * SEG],
                   scr_ktil[t0s:t0s + SEG, k * 128:(k + 1) * 128], transpose=True)
            kb.dma("sync", qT_s[:, k * SEG:(k + 1) * SEG],
                   scr_qbf[t0s:t0s + SEG, k * 128:(k + 1) * 128], transpose=True)
        for mt in range(SEG // 128):
            kb.dma("sync", brow2[:], scr_bmt[t0s + mt * 128:t0s + mt * 128 + 128, :])
            for h in range(16):
                t.transpose(psX[:, 0:128], brow2[:, h * 128:(h + 1) * 128],
                            ident[:])
                v.tensor_copy(bmt_em[:, h * SEG + mt * 128:
                                     h * SEG + mt * 128 + 128], psX[:, 0:128])
        for grp in range(SEG // 64):
            gt0 = grp * 64
            for c in range(4):
                kb.dma("sync", ktp[32 * c:32 * c + 16, :],
                       scr_ktil[t0s + gt0 + 16 * c:t0s + gt0 + 16 * c + 16, :])
            for h in range(16):
                v.tensor_copy(
                    ktilpad[:, h * 128:h * 128 + 128]
                    .rearrange("p (c s) -> p c s", c=4)[:, :, 0:16],
                    ktT_s[:, h * SEG + gt0:h * SEG + gt0 + 64]
                    .rearrange("p (c s) -> p c s", c=4))
                t.matmul(psC[:, 0:64], ktilpad[:, h * 128:h * 128 + 128],
                         qT_s[:, h * SEG + gt0:h * SEG + gt0 + 64],
                         start=True, stop=True)
                v.tensor_tensor(cqT[:, h * 64:(h + 1) * 64], psC[:, 0:64],
                                cmq[:], ALU.mult)
                t.matmul(psC[:, 0:64], ktilpad[:, h * 128:h * 128 + 128],
                         knT_s[:, h * SEG + gt0:h * SEG + gt0 + 64],
                         start=True, stop=True)
                v.tensor_tensor(ckT[:, h * 64:(h + 1) * 64], psC[:, 0:64],
                                cmk[:], ALU.mult)
                pk = psK if h < 8 else psK2
                pq = psQ if h < 8 else psQ2
                ho = (h % 8) * 64
                t.matmul(pk[:, ho:ho + 64], S_bf[:, h * 128:(h + 1) * 128],
                         knT_s[:, h * SEG + gt0:h * SEG + gt0 + 64],
                         start=True, stop=True)
                t.matmul(pq[:, ho:ho + 64], S_bf[:, h * 128:(h + 1) * 128],
                         qT_s[:, h * SEG + gt0:h * SEG + gt0 + 64],
                         start=True, stop=True)
            v.tensor_copy(khb_sb[:, 0:512], psK[:])
            v.tensor_copy(khb_sb[:, 512:1024], psK2[:])
            v.tensor_copy(qhb_sb[:, 0:512], psQ[:])
            v.tensor_copy(qhb_sb[:, 512:1024], psQ2[:])

            for c in range(4):
                co = 16 * c
                khbv = khb_sb[:].rearrange("p (h s) -> p h s", h=16)[:, :, co:co + 16]
                qhbv = qhb_sb[:].rearrange("p (h s) -> p h s", h=16)[:, :, co:co + 16]
                kh3 = kh_w[:].rearrange("p (h s) -> p h s", h=16)
                for h in range(16):
                    t.matmul(psC[:, h * 16:h * 16 + 16],
                             vstk[:, h * 128:(h + 1) * 128],
                             ckT[:, h * 64 + co:h * 64 + co + 16],
                             start=True, stop=True, skip_group_check=True)
                v.tensor_tensor(kh3, khbv,
                                psC[:, 0:256].rearrange("p (h s) -> p h s", h=16),
                                ALU.subtract)
                t.matmul(psR[:, 0:256], inv128c[:], kh_w[:], start=True, stop=True)
                s.activation(sq_t[:], kh_w[:], AF.Square)
                t.matmul(psR[:, 256:512], inv128c[:], sq_t[:], start=True, stop=True)
                v.tensor_copy(rows[:, 0:256], psR[:, 0:256])
                v.tensor_tensor(rows[:, 256:512], rows[:, 0:256], rows[:, 0:256],
                                ALU.mult)
                v.tensor_tensor(rows[:, 256:512], psR[:, 256:512], rows[:, 256:512],
                                ALU.subtract)
                s.activation(rows[:, 256:512], rows[:, 256:512], AF.Ln, bias=1e-6)
                s.activation(rows[:, 256:512], rows[:, 256:512], AF.Exp, scale=-0.5)
                t.matmul(psB[:, 0:256], ones_row[:], rows[:, 0:256],
                         start=True, stop=True)
                t.matmul(psB[:, 256:512], ones_row[:], rows[:, 256:512],
                         start=True, stop=True)
                v.tensor_tensor(khh[:], kh_w[:], psB[:, 0:256], ALU.subtract)
                v.tensor_tensor(khh[:], khh[:], psB[:, 256:512], ALU.mult)
                kh3h = khh[:].rearrange("p (h s) -> p h s", h=16)
                g3 = gg_t[:].rearrange("p (h s) -> p h s", h=16)
                v.tensor_tensor(g3, kh3h, wv3, ALU.mult)
                v.tensor_tensor(g3, g3,
                                seg_chunk_view(bmt_em, SEG, gt0 + co, 16),
                                ALU.add)
                v.tensor_tensor(gw_t[:].rearrange("p (h s) -> p h s", h=16),
                                g3, wv3, ALU.mult)
                v.tensor_tensor(pp_t[:], gw_t[:], khh[:], ALU.mult)
                t.matmul(psR[:, 0:256], inv128c[:], gw_t[:], start=True, stop=True)
                t.matmul(psR[:, 256:512], inv128c[:], pp_t[:], start=True, stop=True)
                v.tensor_copy(rows2[:], psR[:])
                t.matmul(psB[:, 0:256], ones_row[:], rows2[:, 0:256],
                         start=True, stop=True, skip_group_check=True)
                t.matmul(psB[:, 256:512], ones_row[:], rows2[:, 256:512],
                         start=True, stop=True, skip_group_check=True)
                v.tensor_tensor(t1_t[:], gw_t[:], psB[:, 0:256], ALU.subtract)
                v.tensor_tensor(t2_t[:], khh[:], psB[:, 256:512], ALU.mult)
                v.tensor_tensor(t1_t[:], t1_t[:], t2_t[:], ALU.subtract)
                t.matmul(psB[:, 0:256], ones_row[:], rows[:, 256:512],
                         start=True, stop=True, skip_group_check=True)
                v.tensor_tensor(t1_t[:], t1_t[:], psB[:, 0:256], ALU.mult)
                v.tensor_copy(stag[:].rearrange("p (h s) -> p h s", h=16)[:, :, 0:16],
                              t1_t[:].rearrange("p (h s) -> p h s", h=16))
                for tq in range(4):
                    t.transpose(psX[:, tq * 128:(tq + 1) * 128],
                                stag[:, tq * 128:(tq + 1) * 128], ident[:])
                for h in range(16):
                    tq = h // 4
                    ho = 32 * (h % 4)
                    v.tensor_copy(vstk[32 * c:32 * c + 32, h * 128:(h + 1) * 128],
                                  psX[ho:ho + 32, tq * 128:(tq + 1) * 128])
                for h in range(16):
                    t.matmul(psC[:, 256 + h * 16:256 + h * 16 + 16],
                             vstk[:, h * 128:(h + 1) * 128],
                             cqT[:, h * 64 + co:h * 64 + co + 16],
                             start=True, stop=True, skip_group_check=True)
                v.tensor_tensor(kh_w[:].rearrange("p (h s) -> p h s", h=16), qhbv,
                                psC[:, 256:512].rearrange("p (h s) -> p h s", h=16),
                                ALU.subtract)
                t.matmul(psR[:, 0:256], inv128c[:], kh_w[:], start=True, stop=True,
                         skip_group_check=True)
                s.activation(sq_t[:], kh_w[:], AF.Square)
                t.matmul(psR[:, 256:512], inv128c[:], sq_t[:], start=True,
                         stop=True, skip_group_check=True)
                v.tensor_copy(rows[:, 0:256], psR[:, 0:256])
                v.tensor_tensor(rows[:, 256:512], rows[:, 0:256], rows[:, 0:256],
                                ALU.mult)
                v.tensor_tensor(rows[:, 256:512], psR[:, 256:512], rows[:, 256:512],
                                ALU.subtract)
                s.activation(rows[:, 256:512], rows[:, 256:512], AF.Ln, bias=1e-6)
                s.activation(rows[:, 256:512], rows[:, 256:512], AF.Exp, scale=-0.5)
                t.matmul(psB[:, 0:256], ones_row[:], rows[:, 0:256],
                         start=True, stop=True, skip_group_check=True)
                t.matmul(psB[:, 256:512], ones_row[:], rows[:, 256:512],
                         start=True, stop=True, skip_group_check=True)
                v.tensor_tensor(khh[:], kh_w[:], psB[:, 0:256], ALU.subtract)
                v.tensor_tensor(khh[:], khh[:], psB[:, 256:512], ALU.mult)
                do3 = doT[:].rearrange("p (h s) -> p h s", h=16)[:, :,
                                                                gt0 + co:gt0 + co + 16]
                v.tensor_tensor(t2_t[:].rearrange("p (h s) -> p h s", h=16),
                                khh[:].rearrange("p (h s) -> p h s", h=16),
                                wv3, ALU.mult)
                v.tensor_tensor(do3,
                                t2_t[:].rearrange("p (h s) -> p h s", h=16),
                                bv3, ALU.add)
            for h in range(16):
                t.matmul(psX[:, 128 * (h % 4):128 * (h % 4) + 128],
                         ktp[:, h * 128:(h + 1) * 128],
                         vstk[:, h * 128:(h + 1) * 128],
                         start=True, stop=True, skip_group_check=True)
                if h % 4 == 3:
                    for hh in range(h - 3, h + 1):
                        v.tensor_tensor(S_sb[:, hh * 128:(hh + 1) * 128],
                                        S_sb[:, hh * 128:(hh + 1) * 128],
                                        psX[:, 128 * (hh % 4):128 * (hh % 4) + 128],
                                        ALU.subtract)
                        v.tensor_copy(S_bf[:, hh * 128:(hh + 1) * 128],
                                      S_sb[:, hh * 128:(hh + 1) * 128])
        dorow = kb.sbuf("a1_dor%d" % seg, [128, D], F32, scope=sc) \
            if seg == 0 else dorow
        qrow = kb.sbuf("a1_qr%d" % seg, [128, D], F32, scope=sc) \
            if seg == 0 else qrow
        ogst = kb.sbuf("a1_ogs%d" % seg, [128, 40], F32, scope=sc) \
            if seg == 0 else ogst
        ogbf = kb.sbuf("a1_ogb%d" % seg, [128, D], BF16, scope=sc) \
            if seg == 0 else ogbf
        for mt in range(SEG // 128):
            gtok = t0s + mt * 128
            for h in range(16):
                t.transpose(psX[:, 0:128],
                            doT[:, h * SEG + mt * 128:h * SEG + mt * 128 + 128],
                            ident[:])
                v.tensor_copy(dorow[:, h * 128:(h + 1) * 128], psX[:, 0:128])
            kb.dma("sync", qrow[:], scr_q[gtok:gtok + 128, :])
            kb.dma("sync", brow2[:], scr_sg[gtok:gtok + 128, :])
            v.tensor_tensor(dorow[:], dorow[:], qrow[:], ALU.add)
            v.tensor_tensor(dorow[:], dorow[:], brow2[:], ALU.mult)
            s.activation(qrow[:], dorow[:], AF.Square)
            v.tensor_reduce(ogst[:, 0:16],
                            qrow[:].rearrange("p (h e) -> p h e", h=16),
                            AX.X, ALU.add)
            s.activation(ogst[:, 16:32], ogst[:, 0:16], AF.Ln, bias=1e-6,
                         scale=1.0 / 128)
            s.activation(ogst[:, 16:32], ogst[:, 16:32], AF.Exp, scale=-0.5)
            v.tensor_tensor(ogbf[:].rearrange("p (h e) -> p h e", h=16),
                            dorow[:].rearrange("p (h e) -> p h e", h=16),
                            ogst[:, 16:32].unsqueeze(-1)
                            .broadcast_to([128, 16, 128]), ALU.mult)
            kb.dma("scalar", scr_og[gtok:gtok + 128, :], ogbf[:])
    kb.barrier()
    sc.close()

    sc = ExitStack()
    ogT = kb.sbuf("a1_ogT", [128, 16 * T], BF16, scope=sc)
    for k in range(16):
        kb.dma("sync", ogT[:, k * T:(k + 1) * T],
               scr_og[:, k * 128:(k + 1) * 128], transpose=True)
    wo_sb = kb.sbuf("a1_wo", [128, 16 * T], BF16, scope=sc)
    worow = kb.sbuf("a1_worow", [128, D], F32, scope=sc)
    for k in range(16):
        kb.dma("sync", worow[:], W["wo"][k * 128:(k + 1) * 128, :])
        v.tensor_scalar(wo_sb[:, k * T:(k + 1) * T], worow[:], nwc[:, k:k + 1],
                        None, ALU.mult)
    xres = kb.sbuf("a1_xres", [128, D], F32, scope=sc)
    orow = kb.sbuf("a1_orow", [128, D], F32, scope=sc)
    lstat = kb.sbuf("a1_lst", [128, 8], F32, scope=sc)
    ltmp = kb.sbuf("a1_ltmp", [128, D], F32, scope=sc)
    kb.barrier()
    banks = [kb.psum("a1_po%d" % i, [128, 512], F32, scope=sc) for i in range(4)]
    sto = MMStage(kb, banks, "a1o")
    for m in range(16):
        kb.dma("sync", xres[:], x2_in[m * 128:(m + 1) * 128, :])
        for n in range(4):

            def mms(bank, m=m, n=n):
                for k in range(16):
                    t.matmul(bank[:], ogT[:, k * T + m * 128:k * T + m * 128 + 128],
                             wo_sb[:, k * T + n * 512:k * T + n * 512 + 512],
                             start=(k == 0), stop=(k == 15))

            def evac(bank, m=m, n=n):
                v.tensor_tensor(orow[:, n * 512:(n + 1) * 512], bank[:],
                                xres[:, n * 512:(n + 1) * 512], ALU.add)

            sto.tile_custom(mms, evac)
        emit_ln_rows(kb, orow[:], orow[:], D, 1e-5, ltmp, lstat)
        kb.dma("gpsimd", x3_out[m * 128:(m + 1) * 128, :], orow[:])
    kb.barrier()
    sc.close()


def emit_tern_global(kb, w_dram, out_dram, O, I, sc, tag, gcol, eps=1e-6):
    t, v, s = kb.t, kb.v, kb.s
    ones_row = kb.cs["ones_row"]
    wt = kb.sbuf(f"gw_{tag}", [128, I], F32, scope=sc)
    acc = kb.sbuf(f"ga_{tag}", [128, 2], F32, scope=sc)
    tq = kb.sbuf(f"gt_{tag}", [128, I], F32, scope=sc)
    wq = kb.sbuf(f"gq_{tag}", [128, I], BF16, scope=sc)
    pgs = kb.psum(f"gp_{tag}", [128, 8], F32, scope=sc)
    nrt = O // 128
    for rt in range(nrt):
        kb.dma("sync", wt[:], w_dram[rt * 128:(rt + 1) * 128, :])
        v.tensor_reduce(acc[:, 1:2], wt[:], AX.X, ALU.add,
                        apply_absolute_value=True)
        if rt == 0:
            v.tensor_copy(acc[:, 0:1], acc[:, 1:2])
        else:
            v.tensor_tensor(acc[:, 0:1], acc[:, 0:1], acc[:, 1:2], ALU.add)
    t.matmul(pgs[0:1, 0:1], kb.cs["ones_col"][:], acc[:, 0:1],
             start=True, stop=True)
    s.activation(acc[0:1, 1:2], pgs[0:1, 0:1], AF.Copy, scale=1.0 / (O * I))
    t.matmul(pgs[:, 0:1], ones_row[:], acc[0:1, 1:2], start=True, stop=True)
    v.tensor_copy(gcol[:, 0:1], pgs[:, 0:1])
    v.tensor_scalar(gcol[:, 1:2], gcol[:, 0:1], float(eps), 0.5,
                    ALU.add, ALU.mult)
    for rt in range(nrt):
        kb.dma("sync", wt[:], w_dram[rt * 128:(rt + 1) * 128, :])
        v.tensor_scalar(tq[:], wt[:], gcol[:, 1:2], None, ALU.is_ge)
        v.tensor_scalar(wt[:], wt[:], -1.0, gcol[:, 1:2], ALU.mult, ALU.is_ge)
        v.scalar_tensor_tensor(wq[:], wt[:], -1.0, tq[:], ALU.mult, ALU.add)
        kb.dma("scalar", out_dram[rt * 128:(rt + 1) * 128, :], wq[:])


def emit_absmax_quant(kb, xn_ap, out_bf_ap, sc_col, st):
    v = kb.v
    gam, sm = st[:, 0:1], st[:, 1:2]
    v.tensor_reduce(gam, xn_ap, AX.X, ALU.max, apply_absolute_value=True)
    v.reciprocal(sm, gam)
    v.tensor_scalar(sm, sm, 128.0, None, ALU.mult)
    v.tensor_scalar(sc_col, gam, 1.0 / 128.0, None, ALU.mult)
    v.tensor_scalar(out_bf_ap, xn_ap, sm, 128.0, ALU.mult, ALU.min)
    return v.tensor_scalar(out_bf_ap, out_bf_ap, -128.0, None, ALU.max)


def build_l1_ffn(kb, x3_in, w_fc, w_proj, out_dram, cs):
    t, v, s, g = kb.t, kb.v, kb.s, kb.g
    DF = 8192
    dk = "ExternalOutput" if DBG else "Internal"
    scr_xq = kb.dram("f1_xq", [T, D], BF16, kind=dk)
    scr_wfq = kb.dram("f1_wfq", [DF, D], BF16, kind=dk)
    scr_wpq = kb.dram("f1_wpq", [D, DF], BF16, kind=dk)
    scr_h = kb.dram("f1_h", [T, DF], F32, kind=dk)
    scr_hq = kb.dram("f1_hq", [T, DF], BF16, kind=dk)
    scr_pd = kb.dram("f1_pd", [T, D], F32)
    sc1 = kb.sbuf("f1_sc1", [128, 16], F32)
    sc2 = kb.sbuf("f1_sc2", [128, 16], F32)
    gwf = kb.sbuf("f1_gwf", [128, 2], F32)
    gwp = kb.sbuf("f1_gwp", [128, 2], F32)

    sc = ExitStack()
    xrow = kb.sbuf("f1_xrow", [128, D], F32, scope=sc)
    ltmp = kb.sbuf("f1_lt", [128, D], F32, scope=sc)
    lstat = kb.sbuf("f1_ls", [128, 8], F32, scope=sc)
    xq_bf = kb.sbuf("f1_xqb", [128, D], BF16, scope=sc)
    for m in range(16):
        kb.dma("sync", xrow[:], x3_in[m * 128:(m + 1) * 128, :])
        emit_ln_rows(kb, xrow[:], ltmp[:, 0:D], D, 1e-6, ltmp, lstat)
        emit_absmax_quant(kb, ltmp[:, 0:D], xq_bf[:], sc1[:, m:m + 1], lstat)
        kb.dma("scalar", scr_xq[m * 128:(m + 1) * 128, :], xq_bf[:])
    kb.barrier()
    sc.close()

    sc = ExitStack()
    emit_tern_global(kb, w_fc, scr_wfq, DF, D, sc, "wf", gwf)
    kb.barrier()
    sc.close()
    sc = ExitStack()
    emit_tern_global(kb, w_proj, scr_wpq, D, DF, sc, "wp", gwp)
    kb.v.tensor_scalar(sc1[:], sc1[:], gwf[:, 0:1], None, ALU.mult)
    kb.barrier()
    sc.close()

    sc = ExitStack()
    xqT = kb.sbuf("f1_xqT", [128, 16 * T], BF16, scope=sc)
    for k in range(16):
        kb.dma("sync", xqT[:, k * T:(k + 1) * T],
               scr_xq[:, k * 128:(k + 1) * 128], transpose=True)
    wfT = [kb.sbuf("f1_wfT%d" % i, [128, 16 * 512], BF16, scope=sc)
           for i in range(2)]
    hrow = [kb.sbuf("f1_hr%d" % i, [128, 512], F32, scope=sc) for i in range(2)]
    kb.barrier()
    psC = [kb.psum("f1_psC%d" % i, [128, 512], F32, scope=sc) for i in range(4)]
    stc = MMStage(kb, psC, "f1fc")
    for slab in range(16):
        wb = wfT[slab % 2]
        s0 = slab * 512
        for k in range(16):
            kb.dma("sync", wb[:, k * 512:(k + 1) * 512],
                   scr_wfq[s0:s0 + 512, k * 128:(k + 1) * 128], transpose=True)
        for m in range(16):

            def mms(bank, m=m, wb=wb):
                for k in range(16):
                    t.matmul(bank[:],
                             xqT[:, k * T + m * 128:k * T + m * 128 + 128],
                             wb[:, k * 512:(k + 1) * 512],
                             start=(k == 0), stop=(k == 15),
                             skip_group_check=True)

            def evac(bank, m=m, slab=slab):
                hb = hrow[m % 2]
                s.activation(hb[:], bank[:], AF.Gelu, scale=sc1[:, m:m + 1])
                kb.dma("scalar", scr_h[m * 128:(m + 1) * 128,
                                       slab * 512:(slab + 1) * 512], hb[:])
                return None

            stc.tile_custom(mms, evac)
    kb.barrier()
    sc.close()

    sc = ExitStack()
    hrow2 = kb.sbuf("f1_h2", [128, DF], F32, scope=sc)
    ltmp2 = kb.sbuf("f1_lt2", [128, DF], F32, scope=sc)
    lstat2 = kb.sbuf("f1_ls2", [128, 8], F32, scope=sc)
    hq_bf = kb.sbuf("f1_hqb", [128, DF], BF16, scope=sc)
    for m in range(16):
        kb.dma("sync", hrow2[:], scr_h[m * 128:(m + 1) * 128, :])
        emit_ln_rows(kb, hrow2[:], ltmp2[:, 0:DF], DF, 1e-6, ltmp2, lstat2)
        emit_absmax_quant(kb, ltmp2[:, 0:DF], hq_bf[:], sc2[:, m:m + 1], lstat2)
        kb.dma("scalar", scr_hq[m * 128:(m + 1) * 128, :], hq_bf[:])
    kb.v.tensor_scalar(sc2[:], sc2[:], gwp[:, 0:1], None, ALU.mult)
    kb.barrier()
    sc.close()

    KH = [(0, 16), (16, 16), (32, 16), (48, 16)]
    for half, (k0, nk) in enumerate(KH):
        sc = ExitStack()
        hqT = kb.sbuf("f1_hqT%d" % half, [128, nk * T], BF16, scope=sc)
        wpT = kb.sbuf("f1_wpT%d" % half, [128, nk * T], BF16, scope=sc)
        for k in range(nk):
            kb.dma("sync", hqT[:, k * T:(k + 1) * T],
                   scr_hq[:, (k0 + k) * 128:(k0 + k + 1) * 128], transpose=True)
            kb.dma("sync", wpT[:, k * T:(k + 1) * T],
                   scr_wpq[:, (k0 + k) * 128:(k0 + k + 1) * 128], transpose=True)
        kb.barrier()
        banks = [kb.psum("f1_pd%d_%d" % (half, i), [128, 512], F32, scope=sc)
                 for i in range(4)]
        prow = kb.sbuf("f1_prow%d" % half, [128, D], F32, scope=sc)
        orow = kb.sbuf("f1_orow%d" % half, [128, D], F32, scope=sc)
        x3r = kb.sbuf("f1_x3r%d" % half, [128, D], F32, scope=sc)
        lstat3 = kb.sbuf("f1_ls3%d" % half, [128, 8], F32, scope=sc)
        st2 = MMStage(kb, banks, "f1dn%d" % half)
        for m in range(16):
            if half > 0:
                kb.dma("sync", prow[:], scr_pd[m * 128:(m + 1) * 128, :])
            if half == len(KH) - 1:
                kb.dma("sync", x3r[:], x3_in[m * 128:(m + 1) * 128, :])
            for n in range(4):

                def mms(bank, m=m, n=n):
                    for k in range(nk):
                        t.matmul(bank[:],
                                 hqT[:, k * T + m * 128:k * T + m * 128 + 128],
                                 wpT[:, k * T + n * 512:k * T + n * 512 + 512],
                                 start=(k == 0), stop=(k == nk - 1),
                                 skip_group_check=True)

                if half == 0:
                    def evac(bank, m=m, n=n):
                        v.tensor_copy(orow[:, n * 512:(n + 1) * 512], bank[:])
                else:
                    def evac(bank, m=m, n=n):
                        v.tensor_tensor(orow[:, n * 512:(n + 1) * 512],
                                        bank[:], prow[:, n * 512:(n + 1) * 512],
                                        ALU.add)
                st2.tile_custom(mms, evac)
            if half < len(KH) - 1:
                kb.dma("scalar", scr_pd[m * 128:(m + 1) * 128, :], orow[:])
            else:
                v.scalar_tensor_tensor(orow[:], orow[:], sc2[:, m:m + 1],
                                       x3r[:], ALU.mult, ALU.add)
                emit_ln_rows(kb, orow[:], orow[:], D, 1e-5, prow, lstat3)
                kb.dma("gpsimd", out_dram[m * 128:(m + 1) * 128, :], orow[:])
        kb.barrier()
        sc.close()


def make_consts():
    ident = np.eye(128, dtype=np.float32)
    p = np.arange(128)[:, None]
    f = np.arange(512)[None, :]
    masks = [np.where(p + 128 * j > f, NEGBIG, 0.0) for j in range(4)]
    dmask = np.concatenate(masks, axis=1).astype(np.float32)
    eh2 = np.zeros((128, 2), dtype=np.float32)
    eh2[0:64, 0] = 1.0
    eh2[64:128, 1] = 1.0
    selrow = np.zeros((2, 256), dtype=np.float32)
    selrow[0, 0:128] = 1.0
    selrow[1, 128:256] = 1.0
    js = np.arange(128)
    ic = np.arange(64)
    js_tok = 16 * (js // 32) + (js % 32)
    js_real = (js % 32) < 16
    cmq = (js_real[:, None] & (js_tok[:, None] <= ic[None, :])).astype(np.float32)
    cmk = (js_real[:, None] & ((js // 32)[:, None] * 16 + 16 <= ic[None, :])
           ).astype(np.float32)
    return {"c_ident": ident, "c_dmask": dmask, "c_eh2": eh2,
            "c_selrow": selrow, "c_cmq": cmq, "c_cmk": cmk}


def load_consts(kb):
    cs = {}
    dmask_d = kb.dram("c_dmask", [128, 2048], F32, kind="ExternalInput")
    eh2_d = kb.dram("c_eh2", [128, 2], F32, kind="ExternalInput")
    selrow_d = kb.dram("c_selrow", [2, 256], F32, kind="ExternalInput")
    ident_d = kb.dram("c_ident", [128, 128], F32, kind="ExternalInput")
    cs["dmask"] = kb.sbuf("sb_dmask", [128, 2048], F32)
    cs["eh2"] = kb.sbuf("sb_eh2", [128, 2], F32)
    cs["selrow"] = kb.sbuf("sb_selrow", [2, 256], F32)
    cs["ident"] = kb.sbuf("sb_ident", [128, 128], F32)
    kb.dma("sync", cs["dmask"][:], dmask_d)
    kb.dma("sync", cs["eh2"][:], eh2_d)
    kb.dma("sync", cs["selrow"][:], selrow_d)
    kb.dma("sync", cs["ident"][:], ident_d)
    cmq_d = kb.dram("c_cmq", [128, 64], F32, kind="ExternalInput")
    cmk_d = kb.dram("c_cmk", [128, 64], F32, kind="ExternalInput")
    cs["cmq"] = kb.sbuf("sb_cmq", [128, 64], F32)
    cs["cmk"] = kb.sbuf("sb_cmk", [128, 64], F32)
    kb.dma("sync", cs["cmq"][:], cmq_d)
    kb.dma("sync", cs["cmk"][:], cmk_d)
    cs["ones_col"] = kb.sbuf("sb_ones_col", [128, 1], F32)
    cs["ones_colb"] = kb.sbuf("sb_ones_colb", [128, 1], BF16)
    cs["ones_row"] = kb.sbuf("sb_ones_row", [1, 128], F32)
    kb.v.memset(cs["ones_col"][:], 1.0)
    kb.v.memset(cs["ones_colb"][:], 1.0)
    kb.v.memset(cs["ones_row"][:], 1.0)
    pool = kb.sbuf("sb_constf", [128, 16], F32)
    for i, val in enumerate([1e-5, 1e-6, 0.5, 1.0]):
        kb.v.memset(pool[:, i:i + 1], float(val))
        kb.nc.const_aps.register(F32, float(val), pool[:, i:i + 1])
    return cs


def build_l0a_program():
    nc = bass.Bass("TRN2", target_bir_lowering=False)
    kb = KB(nc)
    x_in = kb.dram("x_in", [T, D], F32, kind="ExternalInput")
    wqkv = kb.dram("wqkv", [D, 3 * D], F32, kind="ExternalInput")
    wproj = kb.dram("wproj", [D, D], F32, kind="ExternalInput")
    lam_in = kb.dram("lam_in", [1, 256], F32, kind="ExternalInput")
    x1_out = kb.dram("x1_out", [T, D], F32, kind="ExternalOutput")
    cs = load_consts(kb)
    kb.cs = cs
    kb.barrier()
    build_l0_attn(kb, x_in, wqkv, wproj, lam_in, x1_out, cs)
    return kb.finalize()


def build_l0f_program():
    nc = bass.Bass("TRN2", target_bir_lowering=False)
    kb = KB(nc)
    x1_in = kb.dram("x1_in", [T, D], F32, kind="ExternalInput")
    w_gate = kb.dram("w_gate", [5504, D], F32, kind="ExternalInput")
    w_up = kb.dram("w_up", [5504, D], F32, kind="ExternalInput")
    w_down = kb.dram("w_down", [D, 5504], F32, kind="ExternalInput")
    x2_out = kb.dram("x2_out", [T, D], F32, kind="ExternalOutput")
    cs = load_consts(kb)
    kb.cs = cs
    kb.barrier()
    build_l0_ffn(kb, x1_in, w_gate, w_up, w_down, x2_out, cs)
    return kb.finalize()


def build_l1f_program():
    nc = bass.Bass("TRN2", target_bir_lowering=False)
    kb = KB(nc)
    x3_in = kb.dram("x3_in", [T, D], F32, kind="ExternalInput")
    w_fc = kb.dram("w_fc", [8192, D], F32, kind="ExternalInput")
    w_proj = kb.dram("w_proj", [D, 8192], F32, kind="ExternalInput")
    out_d = kb.dram("out_d", [T, D], F32, kind="ExternalOutput")
    kb.cs = load_consts(kb)
    kb.barrier()
    build_l1_ffn(kb, x3_in, w_fc, w_proj, out_d, kb.cs)
    return kb.finalize()


def build_l1a_program():
    nc = bass.Bass("TRN2", target_bir_lowering=False)
    kb = KB(nc)
    x2_in = kb.dram("x2_in", [T, D], F32, kind="ExternalInput")
    W = {"wq": kb.dram("wq", [D, D], F32, kind="ExternalInput"),
         "wk": kb.dram("wk", [D, D], F32, kind="ExternalInput"),
         "wv": kb.dram("wv", [D, D], F32, kind="ExternalInput"),
         "wo": kb.dram("wo", [D, D], F32, kind="ExternalInput"),
         "wbeta": kb.dram("wbeta", [D, 16], F32, kind="ExternalInput"),
         "ln_w": kb.dram("ln_w", [16, 128], F32, kind="ExternalInput"),
         "ln_b": kb.dram("ln_b", [16, 128], F32, kind="ExternalInput"),
         "wg1": kb.dram("wg1", [D, 128], F32, kind="ExternalInput"),
         "wg2": kb.dram("wg2", [128, D], F32, kind="ExternalInput"),
         "norm_w": kb.dram("norm_w", [1, D], F32, kind="ExternalInput")}
    x3_out = kb.dram("x3_out", [T, D], F32, kind="ExternalOutput")
    kb.cs = load_consts(kb)
    kb.barrier()
    build_l1_attn(kb, x2_in, W, x3_out, kb.cs)
    return kb.finalize()


def build_fused_program():
    nc = bass.Bass("TRN2", target_bir_lowering=False)
    kb = KB(nc)
    x_in = kb.dram("x_in", [T, D], F32, kind="ExternalInput")
    wqkv = kb.dram("wqkv", [D, 3 * D], F32, kind="ExternalInput")
    wproj = kb.dram("wproj", [D, D], F32, kind="ExternalInput")
    lam_in = kb.dram("lam_in", [1, 256], F32, kind="ExternalInput")
    w_gate = kb.dram("w_gate", [5504, D], F32, kind="ExternalInput")
    w_up = kb.dram("w_up", [5504, D], F32, kind="ExternalInput")
    w_down = kb.dram("w_down", [D, 5504], F32, kind="ExternalInput")
    W1 = {"wq": kb.dram("wq", [D, D], F32, kind="ExternalInput"),
          "wk": kb.dram("wk", [D, D], F32, kind="ExternalInput"),
          "wv": kb.dram("wv", [D, D], F32, kind="ExternalInput"),
          "wo": kb.dram("wo", [D, D], F32, kind="ExternalInput"),
          "wbeta": kb.dram("wbeta", [D, 16], F32, kind="ExternalInput"),
          "ln_w": kb.dram("ln_w", [16, 128], F32, kind="ExternalInput"),
          "ln_b": kb.dram("ln_b", [16, 128], F32, kind="ExternalInput"),
          "wg1": kb.dram("wg1", [D, 128], F32, kind="ExternalInput"),
          "wg2": kb.dram("wg2", [128, D], F32, kind="ExternalInput"),
          "norm_w": kb.dram("norm_w", [1, D], F32, kind="ExternalInput")}
    w_fc = kb.dram("w_fc", [8192, D], F32, kind="ExternalInput")
    w_proj1 = kb.dram("w_proj", [D, 8192], F32, kind="ExternalInput")
    out_d = kb.dram("out_d", [T, D], F32, kind="ExternalOutput")
    x1_s = kb.dram("x1_s", [T, D], F32)
    x2_s = kb.dram("x2_s", [T, D], F32)
    x3_s = kb.dram("x3_s", [T, D], F32)
    poolA = kb.dram("poolA", [T, 8192], F32)
    poolB = kb.dram("poolB", [T * 8192], BF16)
    poolC = kb.dram("poolC", [8192, D], BF16)
    poolD = kb.dram("poolD", [D * 8192], BF16)
    poolE = kb.dram("poolE", [T, D], F32)

    def bview(lo, rows, cols):
        return poolB[lo:lo + rows * cols].rearrange("(a b) -> a b", b=cols)

    M4 = T * D
    kb.scratch_alias = {
        "f0_hid": poolA[:, 0:5504],
        "a1_q": poolA[:, 0:2048], "a1_knf": poolA[:, 2048:4096],
        "a1_bmt": poolA[:, 4096:6144], "a1_sg": poolA[:, 6144:8192],
        "f1_h": poolA[:, :],
        "scr_qkT": bview(0, 4096, T), "scr_vbf": bview(2 * M4, T, D),
        "scr_difT": bview(3 * M4, D, T),
        "f0_xq": bview(0, T, D), "f0_hq": bview(M4, T, 5504),
        "a1_x2bf": bview(0, T, D), "a1_qbf": bview(1 * M4, T, D),
        "a1_kn": bview(2 * M4, T, D), "a1_ktil": bview(3 * M4, T, D),
        "f1_xq": bview(0, T, D), "f1_hq": bview(0, T, 8192),
        "f0_wgq": poolC[0:5504, :], "f1_wfq": poolC[:, :],
        "f0_wuq": poolD[0:5504 * D].rearrange("(a b) -> a b", b=D),
        "f1_wpq": poolD[:].rearrange("(a b) -> a b", b=8192),
        "f0_pd": poolE[:, :], "f1_pd": poolE[:, :],
    }
    kb.cs = load_consts(kb)
    kb.barrier()
    build_l0_attn(kb, x_in, wqkv, wproj, lam_in, x1_s, kb.cs)
    build_l0_ffn(kb, x1_s, w_gate, w_up, w_down, x2_s, kb.cs)
    build_l1_attn(kb, x2_s, W1, x3_s, kb.cs)
    build_l1_ffn(kb, x3_s, w_fc, w_proj1, out_d, kb.cs)
    return kb.finalize()


_NCORES = 8


def _run(nc, im_base, per_core, outname):
    in_maps = []
    for i in range(_NCORES):
        d = dict(im_base)
        d.update({k: v[i] for k, v in per_core.items()})
        in_maps.append(d)
    res = run_bass_kernel_spmd(nc, in_maps, core_ids=list(range(_NCORES)))
    return np.stack([r[outname] for r in res.results], axis=0)


def kernel(**inputs):
    consts = make_consts()
    lam_in = np.concatenate([
        np.asarray(inputs["L0_attn_lambda_q1"]),
        np.asarray(inputs["L0_attn_lambda_k1"]),
        np.asarray(inputs["L0_attn_lambda_q2"]),
        np.asarray(inputs["L0_attn_lambda_k2"])]).reshape(1, 256) \
        .astype(np.float32)
    x = np.ascontiguousarray(np.asarray(inputs["L0_attn_x"], dtype=np.float32))
    nc = build_fused_program()
    im = dict(consts)
    im.update({
        "wqkv": np.asarray(inputs["L0_attn_Wqkv"]),
        "wproj": np.asarray(inputs["L0_attn_Wproj"]),
        "lam_in": lam_in,
        "w_gate": np.asarray(inputs["L0_ffn_w_gate"]),
        "w_up": np.asarray(inputs["L0_ffn_w_up"]),
        "w_down": np.asarray(inputs["L0_ffn_w_down"]),
        "wq": np.asarray(inputs["L1_attn_Wq"]),
        "wk": np.asarray(inputs["L1_attn_Wk"]),
        "wv": np.asarray(inputs["L1_attn_Wv"]),
        "wo": np.asarray(inputs["L1_attn_Wo"]),
        "wbeta": np.asarray(inputs["L1_attn_Wbeta"]),
        "ln_w": np.asarray(inputs["L1_attn_ln_w"]),
        "ln_b": np.asarray(inputs["L1_attn_ln_b"]),
        "wg1": np.asarray(inputs["L1_attn_Wg1"]),
        "wg2": np.asarray(inputs["L1_attn_Wg2"]),
        "norm_w": np.asarray(inputs["L1_attn_norm_w"]).reshape(1, D),
        "w_fc": np.asarray(inputs["L1_ffn_w_fc"]),
        "w_proj": np.asarray(inputs["L1_ffn_w_proj"])})
    return _run(nc, im, {"x_in": x}, "out_d")
